# version 40
# baseline (speedup 1.0000x reference)
import math
import contextlib
import numpy as np
import concourse.bass as bass
import concourse.mybir as mybir
from concourse.bass_utils import run_bass_kernel_spmd

F32 = mybir.dt.float32
BF16 = mybir.dt.bfloat16
I32 = mybir.dt.int32
AF = mybir.ActivationFunctionType
OP = mybir.AluOpType
AXX = mybir.AxisListType.X

T = 256
NT = 2048 // T
DC = 8
GC = T // 128

TWO_PI = 2.0 * math.pi
B_OFF = 256.0
CLAMP = float(np.nextafter(np.float32(np.pi), np.float32(0.0)))


def _cody_consts():
    u = np.float32(TWO_PI).view(np.uint32)
    c1 = np.uint32(u & np.uint32(0xFFFFF000)).view(np.float32)
    r = TWO_PI - float(c1)
    u2 = np.float32(r).view(np.uint32)
    c2 = np.uint32(u2 & np.uint32(0xFFFFF000)).view(np.float32)
    c3 = TWO_PI - float(c1) - float(c2)
    return float(c1), float(c2), float(c3)


C1, C2, C3 = _cody_consts()


class _Sched:
    ENG_SEM = {"tensor": "pe", "vector": "v", "scalar": "a",
               "gpsimd": "g", "sync": "s"}

    def __init__(self):
        self.items = []
        self.cnt = {}
        self.pending = {}

    def cur(self, sem):
        return self.cnt.get(sem, 0)

    def em(self, eng, fn, waits=(), sem=None, amt=1):
        if sem is None:
            sem = self.ENG_SEM[eng]
        w = dict(self.pending.pop(eng, {}))
        for s, v in waits:
            if v > 0:
                w[s] = max(w.get(s, 0), v)
        self.cnt[sem] = self.cnt.get(sem, 0) + amt
        self.items.append((eng, tuple(w.items()), fn, sem, amt))
        return self.cnt[sem]

    def barrier(self, engines=("tensor", "vector", "scalar", "gpsimd")):
        snap = {s: self.cnt.get(s, 0)
                for s in ("pe", "v", "a", "g") if self.cnt.get(s, 0) > 0}
        for e in engines:
            self.pending[e] = dict(snap)


def build_program():
    nc = bass.Bass("TRN2", target_bir_lowering=False)

    d_x = nc.dram_tensor("x", [1024, 2048], F32, kind="ExternalInput")
    d_wmag = nc.dram_tensor("wmag", [8, 128, 1024], F32, kind="ExternalInput")
    d_wq = nc.dram_tensor("wqm", [8, 128, 1024], F32, kind="ExternalInput")
    d_bw = nc.dram_tensor("bwm", [8, 128, 1024], F32, kind="ExternalInput")
    d_w1 = nc.dram_tensor("w1m", [16, 128, 4096], F32, kind="ExternalInput")
    d_w2 = nc.dram_tensor("w2m", [8, 128, 2048], F32, kind="ExternalInput")
    d_la = nc.dram_tensor("lam", [128, 1024], F32, kind="ExternalInput")
    d_lb = nc.dram_tensor("lbm", [128, 1024], F32, kind="ExternalInput")
    d_rw = nc.dram_tensor("rwm", [128, 64], F32, kind="ExternalInput")
    d_bmag = nc.dram_tensor("bmag", [128, 8], F32, kind="ExternalInput")
    d_bq = nc.dram_tensor("bq", [128, 8], F32, kind="ExternalInput")
    d_b1 = nc.dram_tensor("b1", [128, 16], F32, kind="ExternalInput")
    d_b2 = nc.dram_tensor("b2", [128, 8], F32, kind="ExternalInput")
    d_bb = nc.dram_tensor("bb", [128, 8], F32, kind="ExternalInput")
    d_lg = nc.dram_tensor("lg", [128, 32], F32, kind="ExternalInput")
    d_lbt = nc.dram_tensor("lbt", [128, 32], F32, kind="ExternalInput")
    d_rb = nc.dram_tensor("rb", [8, 1], F32, kind="ExternalInput")
    d_ph = nc.dram_tensor("ph", [128, 8], F32, kind="ExternalInput")
    d_i128 = nc.dram_tensor("i128", [128, 128], F32, kind="ExternalInput")
    d_i8 = nc.dram_tensor("i8", [8, 8], F32, kind="ExternalInput")
    d_rep = nc.dram_tensor("rep", [8, 128], F32, kind="ExternalInput")
    d_out = nc.dram_tensor("o", [1024, 2048], F32, kind="ExternalOutput")

    es = contextlib.ExitStack()
    sb = lambda name, shape, dt=F32: es.enter_context(nc.sbuf_tensor(name, shape, dt))
    psb = lambda name: es.enter_context(nc.psum_tensor(name, [128, 512], F32))

    U = 2048
    ux = sb("ux", [128, U]); ux2 = sb("ux2", [128, U]); uxn = sb("uxn", [128, U])
    uxn2 = sb("uxn2", [128, U])
    umag = sb("umag", [128, U]); uqo = sb("uqo", [128, U]); uphi = sb("uphi", [128, U])
    ukf = sb("ukf", [128, U]); upr = sb("upr", [128, U])
    ua = sb("ua", [128, U]); ub = sb("ub", [128, U])
    umr = sb("umr", [128, U]); umi = sb("umi", [128, U]); uac = sb("uac", [128, U])
    ut1 = sb("ut1", [128, U]); ut2 = sb("ut2", [128, U])
    uki = sb("uki", [128, 128], I32)
    uge = sb("uge", [128, 16 * T])
    iota = sb("iota", [128, T])
    sc1 = sb("sc1", [128, T])
    wmb = [sb("wmb%d" % i, [128, 512]) for i in range(3)]
    wqb = [sb("wqb%d" % i, [128, 512]) for i in range(3)]
    bwb = [sb("bwb%d" % i, [128, 512]) for i in range(3)]
    w1b = [sb("w1b%d" % i, [128, 1024]) for i in range(5)]
    w2b = [sb("w2b%d" % i, [128, 512]) for i in range(3)]
    la_s = sb("la_s", [128, 1024]); lb_s = sb("lb_s", [128, 1024])
    rw_s = sb("rw_s", [128, 64])
    stat = sb("stat", [1, 768])
    carry = sb("carry", [128, 24])
    mu4b = sb("mu4b", [128, T]); is4b = sb("is4b", [128, T])
    i128s = sb("i128s", [128, 128]); i8s = sb("i8s", [8, 8]); reps = sb("reps", [8, 128])
    bmag_s = sb("bmag_s", [128, 8]); bq_s = sb("bq_s", [128, 8])
    b1_s = sb("b1_s", [128, 16]); b2_s = sb("b2_s", [128, 8]); bb_s = sb("bb_s", [128, 8])
    lg_s = sb("lg_s", [128, 32]); lbt_s = sb("lbt_s", [128, 32])
    rb_s = sb("rb_s", [8, 1]); ph_s = sb("ph_s", [128, 8]); phm = sb("phm", [128, 8])
    lgf = sb("lgf", [8, T]); ax_s = sb("ax_s", [128, T])
    we_s = lgf
    ltm = sb("ltm", [128, 8]); srt = sb("srt", [128, 8]); msk = sb("msk", [128, 8])
    expt = sb("expt", [128, 8]); wtm = sb("wtm", [128, 8])
    nm = sb("nm", [128, 1]); den = sb("den", [128, 1]); wiv = sb("wiv", [128, 1])
    ones = sb("ones", [128, 1]); onesr = sc1
    cbias = sb("cbias", [128, 4])

    ps = [psb("ps%d" % i) for i in range(8)]

    sem_names = ["pe", "v", "a", "g", "s", "o0", "o1", "xq0", "xq1",
                 "wm0", "wm1", "wm2", "wq0", "wq1", "wq2", "bw0", "bw1", "bw2",
                 "w10", "w11", "w12", "w13", "w14", "w15", "w16", "w17", "w20", "w21", "w22", "w23", "cst"]
    sems = {n: es.enter_context(nc.semaphore(n)) for n in sem_names}

    S = _Sched()

    def dma(q_sem, dst_ap, src_ap, waits=()):
        return S.em("sync", lambda e, d=dst_ap, s=src_ap: e.dma_start(d, s),
                    waits=waits, sem=q_sem, amt=16)

    def gdma(dst_ap, src_ap, waits=(), sem="o0"):
        return S.em("gpsimd", lambda e, d=dst_ap, s=src_ap: e.dma_start(d, s),
                    waits=waits, sem=sem, amt=16)

    def mm(out_ap, lhs_ap, rhs_ap, start, stop, waits=(), tp=None):
        return S.em("tensor",
                    lambda e, o=out_ap, l=lhs_ap, r=rhs_ap, st=start, sp=stop, tp=tp:
                    e.matmul(o, l, r, start=st, stop=sp, tile_position=tp),
                    waits=waits)

    def tpose(out_ap, in_ap, ident_ap, waits=()):
        return S.em("tensor",
                    lambda e, o=out_ap, i=in_ap, d=ident_ap:
                    e.transpose(o, i, d), waits=waits)

    def act(out_ap, in_ap, func, bias=0.0, scale=1.0, waits=()):
        return S.em("scalar",
                    lambda e, o=out_ap, i=in_ap, f=func, b=bias, sc=scale:
                    e.activation(o, i, f, bias=b, scale=sc), waits=waits)

    def vop(fn, waits=()):
        return S.em("vector", fn, waits=waits)

    R_MU, R_TMP, R_VAR = slice(0, T), slice(T, 2*T), slice(2*T, 3*T)
    R_SD, R_ISTD = R_TMP, R_VAR

    bank_free = {b: [] for b in range(8)}
    BIGS = [0, 1, 2, 3, 4]
    SMALLS = [5, 6, 7]
    rot = {"big": [0], "small": [0]}

    def _take(pool, key):
        r = rot[key]
        for i in range(len(pool)):
            b = pool[(r[0] + i) % len(pool)]
            if bank_free[b] is not None:
                r[0] = (r[0] + i + 1) % len(pool)
                w = bank_free[b]
                bank_free[b] = None
                return b, w
        raise RuntimeError(f"no released {key} PSUM bank available")

    def take_big():
        return _take(BIGS, "big")

    def take_small():
        return _take(SMALLS, "small")

    cst_target = 0
    const_pairs = [
        (la_s[:, :], d_la[:, :]), (lb_s[:, :], d_lb[:, :]), (rw_s[:, :], d_rw[:, :]),
        (bmag_s[:, :], d_bmag[:, :]), (bq_s[:, :], d_bq[:, :]),
        (b1_s[:, :], d_b1[:, :]), (b2_s[:, :], d_b2[:, :]), (bb_s[:, :], d_bb[:, :]),
        (lg_s[:, :], d_lg[:, :]), (lbt_s[:, :], d_lbt[:, :]),
        (rb_s[:, :], d_rb[:, :]), (ph_s[:, :], d_ph[:, :]),
        (i128s[:, :], d_i128[:, :]), (i8s[:, :], d_i8[:, :]), (reps[:, :], d_rep[:, :]),
    ]
    for dst, src in const_pairs:
        cst_target = dma("cst", dst, src)

    for i, val in enumerate([1e-5, 1e-8, 5.0, 0.0]):
        S.em("vector", lambda e, i=i, v=val: e.memset(cbias[:, i:i+1], v))
        nc.const_aps.register(F32, val, cbias[:, i:i+1])
    S.em("vector", lambda e: e.memset(sc1[:, :], 1.0))
    S.em("vector", lambda e: e.memset(ones[:, :], 1.0))
    S.em("vector", lambda e: e.memset(carry[:, :], 0.0))
    S.em("vector", lambda e: e.tensor_scalar(phm[:, :], ph_s[:, :], 0.01, None, OP.mult),
         waits=[("cst", cst_target)])
    S.barrier()

    last_use = {}
    S_DEPTH = {"w1": 5, "w2": 3, "wm": 3, "wq": 3, "bw": 3}

    def stream_dma(stream, idx, dst, src):
        par = idx % S_DEPTH[stream]
        q = stream + str(par)
        waits = []
        lu = last_use.get((stream, par))
        if lu:
            waits.append(lu)
        return dma(q, dst, src, waits=waits)

    def prologue_a_dma(j, io_extra=()):
        t0j = j * T
        uxj = ux if j % 2 == 0 else ux2
        xw = [("o" + str(j % 2), 16 * ((j - j % 2) // 2))] if j > 1 else []
        srcap = bass.AP(d_x, t0j, [[2048, 128], [128 * 2048, 8], [1, T]])
        dstap = bass.AP(uxj, 0, [[U, 128], [T, 8], [1, T]])
        xqs = "xq%d" % (j % 2)
        x_dma = dma(xqs, dstap, srcap, waits=xw)
        g_iota = S.em("gpsimd", lambda e, t0j=t0j: e.iota(
            iota[:, :], [[1, T]], base=t0j, channel_multiplier=0,
            allow_small_or_imprecise_dtypes=True), waits=list(io_extra))
        if j == 0:
            for h in range(3):
                stream_dma("wm", h, wmb[h][:, :],
                           d_wmag[h // 2, :, (h % 2)*512:(h % 2 + 1)*512])
                stream_dma("wq", h, wqb[h][:, :],
                           d_wq[h // 2, :, (h % 2)*512:(h % 2 + 1)*512])
        if j == 0:
            for h in range(5):
                stream_dma("w1", h, w1b[h][:, :],
                           d_w1[h // 4, :, (h % 4)*1024:(h % 4 + 1)*1024])
            for h in range(3):
                stream_dma("w2", h, w2b[h][:, :], d_w2[0, :, h*512:(h+1)*512])
        if j == 0:
            for h in range(3):
                stream_dma("bw", h, bwb[h][:, :],
                           d_bw[h // 2, :, (h % 2)*512:(h % 2 + 1)*512])
        return {"x_dma": x_dma, "g_iota": g_iota, "uxj": uxj, "j": j,
                "xqs": xqs}

    def prologue_sq(pa, uge_hz=()):
        pa["a_sq"] = act(uge[:, 0:U], pa["uxj"][:, :], AF.Square,
                         waits=[(pa["xqs"], pa["x_dma"])] + list(uge_hz))

    def prologue_mms(pa):
        uxj = pa["uxj"]; x_dma = pa["x_dma"]
        bS, wS = take_small()
        for k in range(DC):
            w = list(wS) + [(pa["xqs"], x_dma)] if k == 0 else []
            mm(ps[bS][0:1, 0:T], ones[:, 0:1], uxj[:, k*T:(k+1)*T],
               k == 0, k == DC-1, waits=w)
        pa["bS"] = bS
        pa["pe_sum"] = S.cur("pe")
        bQ, wQ = take_big()
        for k in range(DC):
            w = list(wQ) + [("a", pa["a_sq"])] if k == 0 else []
            mm(ps[bQ][32:33, 0:T], ones[:, 0:1], uge[:, k*T:(k+1)*T],
               k == 0, k == DC-1, waits=w, tp=(0, 32))
        pa["bQ"] = bQ
        pa["pe_sumsq"] = S.cur("pe")

    def prologue_b2(pa, row_extra=()):
        bS = pa["bS"]; bQ = pa["bQ"]
        pe_sum = pa["pe_sum"]; pe_sumsq = pa["pe_sumsq"]
        v1 = vop(lambda e, b=bS: e.tensor_scalar(stat[0:1, R_MU], ps[b][0:1, 0:T],
                                                 1.0/1024.0, None, OP.mult),
                 waits=[("pe", pe_sum)] + list(row_extra))
        v2 = vop(lambda e: e.tensor_tensor(stat[0:1, R_TMP], stat[0:1, R_MU],
                                           stat[0:1, R_MU], op=OP.mult),
                 waits=[("v", v1)])
        v3 = vop(lambda e, b=bQ: e.scalar_tensor_tensor(stat[0:1, R_VAR],
                                                        ps[b][32:33, 0:T],
                                                        1.0/1024.0, stat[0:1, R_TMP],
                                                        op0=OP.mult, op1=OP.subtract),
                 waits=[("pe", pe_sumsq), ("v", v2)])
        bank_free[bS] = [("v", v1)]
        bank_free[bQ] = [("v", v3)]
        a1 = act(stat[0:1, R_SD], stat[0:1, R_VAR], AF.Ln, bias=1e-5,
                 waits=[("v", v3)])
        a1b = act(stat[0:1, R_ISTD], stat[0:1, R_SD], AF.Exp, scale=-0.5,
                  waits=[("a", a1)])
        bM, wM = take_small()
        mm(ps[bM][:, 0:T], onesr[0:1, 0:128], stat[0:1, R_MU], True, True,
           waits=list(wM) + [("v", v2)])
        pe_bcm = S.cur("pe")
        bI, wI = take_small()
        mm(ps[bI][:, 0:T], onesr[0:1, 0:128], stat[0:1, R_ISTD], True, True,
           waits=list(wI) + [("a", a1b)])
        pe_bci = S.cur("pe")
        return {"x_dma": pa["x_dma"], "g_iota": pa["g_iota"],
                "pe_sumsq": pe_sumsq,
                "pe_bcm": pe_bcm, "pe_bci": pe_bci, "bM": bM, "bI": bI,
                "uxj": pa["uxj"], "j": pa["j"]}

    def emit_xn(pro_j, xn_extra=()):
        uxj = pro_j["uxj"]
        bM = pro_j["bM"]; bI = pro_j["bI"]
        pe_bcm = pro_j["pe_bcm"]; pe_bci = pro_j["pe_bci"]
        dst = uxn if pro_j["j"] % 2 == 0 else uxn2
        for k in range(DC):
            ks = slice(k*T, (k+1)*T)
            w = []
            if k == 0:
                w = [("pe", pe_bcm)] + list(xn_extra)
            S.em("vector", lambda e, ks=ks, uxj=uxj, dst=dst, b=bM: e.tensor_tensor(
                dst[:, ks], uxj[:, ks], ps[b][:, 0:T], op=OP.subtract), waits=w)
        v_sub = S.cur("v")
        for k in range(DC):
            ks = slice(k*T, (k+1)*T)
            w = [("v", v_sub), ("pe", pe_bci)] if k == 0 else []
            S.em("vector", lambda e, ks=ks, dst=dst, b=bI: e.tensor_tensor(
                dst[:, ks], dst[:, ks], ps[b][:, 0:T], op=OP.mult), waits=w)
        v_xn_ = S.cur("v")
        bank_free[bM] = [("v", v_xn_)]
        bank_free[bI] = [("v", v_xn_)]
        return v_xn_, 0

    def emit_phi(pe_sumsq_j, waits0=(), kf_extra=()):
        for k in range(DC):
            ks = slice(k*T, (k+1)*T)
            w = list(waits0) if k == 0 else []
            vop(lambda e, ks=ks, k=k: e.tensor_scalar(
                uphi[:, ks], iota[:, :], phm[:, k:k+1], None, OP.mult), waits=w)
        v_phi = S.cur("v")
        v5 = vop(lambda e: e.tensor_scalar(umi[:, :], uphi[:, :], 1.0/TWO_PI,
                                           B_OFF + 0.5, OP.mult, OP.add),
                 waits=[("v", v_phi)])
        prev = v5
        for k in range(DC):
            for hh in (0, 1):
                ks = slice(k*T + hh*128, k*T + (hh+1)*128)
                v6k = vop(lambda e, ks=ks: e.tensor_copy(uki[:, :], umi[:, ks]),
                          waits=[("v", prev)])
                prev = vop(lambda e, ks=ks: e.tensor_copy(umi[:, ks], uki[:, :]),
                           waits=[("v", v6k)])
        v8 = vop(lambda e: e.tensor_scalar(umi[:, :], umi[:, :], -B_OFF, None,
                                           OP.add), waits=[("v", prev)])
        c1_ = vop(lambda e: e.scalar_tensor_tensor(ukf[:, :], umi[:, :], -C1,
                                                   uphi[:, :], op0=OP.mult,
                                                   op1=OP.add),
                  waits=[("v", v8)] + list(kf_extra))
        c2_ = vop(lambda e: e.scalar_tensor_tensor(uphi[:, :], umi[:, :], -C2,
                                                   ukf[:, :], op0=OP.mult,
                                                   op1=OP.add),
                  waits=[("v", c1_)])
        c3_ = vop(lambda e: e.scalar_tensor_tensor(ukf[:, :], umi[:, :], -C3,
                                                   uphi[:, :], op0=OP.mult,
                                                   op1=OP.add),
                  waits=[("v", c2_)])
        u1 = vop(lambda e: e.tensor_scalar(umi[:, :], ukf[:, :], -math.pi, None,
                                           OP.is_lt), waits=[("v", c3_)])
        u2 = vop(lambda e: e.scalar_tensor_tensor(umi[:, :], ukf[:, :], math.pi,
                                                  umi[:, :], op0=OP.is_gt,
                                                  op1=OP.subtract),
                 waits=[("v", u1)])
        w4w = vop(lambda e: e.scalar_tensor_tensor(uphi[:, :], umi[:, :], -TWO_PI,
                                                   ukf[:, :], op0=OP.mult,
                                                   op1=OP.add),
                  waits=[("v", u2)])
        d1 = vop(lambda e: e.tensor_scalar(umi[:, :], uphi[:, :], math.pi/2, None,
                                           OP.is_gt), waits=[("v", w4w)])
        d2 = vop(lambda e: e.scalar_tensor_tensor(ukf[:, :], umi[:, :], -TWO_PI,
                                                  uphi[:, :], op0=OP.mult,
                                                  op1=OP.add), waits=[("v", d1)])
        d3 = vop(lambda e: e.tensor_scalar(ukf[:, :], ukf[:, :], math.pi/2, None,
                                           OP.add), waits=[("v", d2)])
        v13 = vop(lambda e: e.tensor_scalar(umi[:, :], uphi[:, :], CLAMP, -CLAMP,
                                            OP.min, OP.max), waits=[("v", w4w)])
        v14 = vop(lambda e: e.tensor_scalar(ukf[:, :], ukf[:, :], CLAMP, -CLAMP,
                                            OP.min, OP.max), waits=[("v", d3)])
        a_sin = act(umr[:, :], umi[:, :], AF.Sin,
                    waits=[("v", v13), ("pe", pe_sumsq_j)])
        a_cos = act(ukf[:, :], ukf[:, :], AF.Sin, waits=[("v", v14)])
        return {"v_phi": v_phi, "w4w": w4w, "a_sin": a_sin, "a_cos": a_cos,
                "v_done": S.cur("v")}

    pro = None
    xn_c = None
    xn_hazard = {0: [], 1: []}
    pe_w2_prev = 0
    for t in range(NT):
        t0 = t * T
        uxp = ux if t % 2 == 0 else ux2
        uxn_t = uxn if t % 2 == 0 else uxn2
        if t == 0:
            pa0 = prologue_a_dma(0)
            prologue_sq(pa0)
            prologue_mms(pa0)
            pro = prologue_b2(pa0)
            xn_c = emit_xn(pro)
            phi_c = emit_phi(pro["pe_sumsq"],
                             waits0=[("g", pro["g_iota"]), ("cst", cst_target)])
        x_dma = pro["x_dma"]; g_iota = pro["g_iota"]; pe_sumsq = pro["pe_sumsq"]
        v_xn, g_xn = xn_c

        def proj_loop(dram, bufs, stream, out_unit, func, bias_sb, dma_w):
            evacs = []
            for m in range(DC):
                bank, bw_ = take_big()
                for h in (0, 1):
                    idx = 2 * m + h
                    par = (t * 16 + idx) % 3
                    waits = [(stream + str(par), dma_w[par])]
                    if h == 0:
                        waits += list(bw_)
                    if idx == 0:
                        waits.append(("v", v_xn))
                        waits.append(("g", g_xn))
                    for kk in range(4):
                        k = h * 4 + kk
                        mm(ps[bank][:, 0:T], bufs[par][:, kk*128:(kk+1)*128],
                           uxn_t[:, k*T:(k+1)*T], k == 0, k == DC-1,
                           waits=waits if kk == 0 else ())
                    pe_g = S.cur("pe")
                    last_use[(stream, par)] = ("pe", pe_g)
                    nidx = idx + 3
                    npp = (t * 16 + nidx) % 3
                    if nidx < 16:
                        nm_, nh = nidx // 2, nidx % 2
                        dma_w[npp] = stream_dma(
                            stream, npp, bufs[npp][:, :],
                            dram[nm_, :, nh*512:(nh+1)*512])
                    elif t + 1 < NT and nidx < 19:
                        nh = nidx - 16
                        dma_w[npp] = stream_dma(
                            stream, npp, bufs[npp][:, :],
                            dram[nh // 2, :, (nh % 2)*512:(nh % 2 + 1)*512])
                ev = act(out_unit[:, m*T:(m+1)*T], ps[bank][:, 0:T],
                         func, bias=bias_sb[:, m:m+1], waits=[("pe", S.cur("pe"))])
                bank_free[bank] = [("a", ev)]
                evacs.append(ev)
            return evacs

        mag_ev = proj_loop(d_wmag, wmb, "wm", umag, AF.Sigmoid, bmag_s,
                           {i: S.cur("wm%d" % i) for i in range(3)})
        qoff_ev = proj_loop(d_wq, wqb, "wq", uqo, AF.Identity, bq_s,
                            {i: S.cur("wq%d" % i) for i in range(3)})
        a_mag = mag_ev[-1]
        a_qoff = qoff_ev[-1]

        bAX, wAX = take_small()
        for k in range(DC):
            w = list(wAX) + [("v", v_xn), ("g", g_xn)] if k == 0 else []
            mm(ps[bAX][:, 0:T], la_s[:, k*128:(k+1)*128], uxn_t[:, k*T:(k+1)*T],
               k == 0, k == DC-1, waits=w)
        pe_ax = S.cur("pe")
        bRT, wRT = take_small()
        for k in range(DC):
            w = list(wRT) + [("v", v_xn), ("g", g_xn)] if k == 0 else []
            mm(ps[bRT][0:8, 0:T], rw_s[:, k*8:(k+1)*8], uxn_t[:, k*T:(k+1)*T],
               k == 0, k == DC-1, waits=w)
        pe_rt = S.cur("pe")
        a_rt = act(lgf[:, :], ps[bRT][0:8, 0:T], AF.Identity, bias=rb_s[0:8, 0:1],
                   waits=[("pe", pe_rt)])
        bank_free[bRT] = [("a", a_rt)]
        tp_pe = []
        tp_banks = []
        for gi in range(GC):
            gs = slice(gi * 128, (gi + 1) * 128)
            bT_, wT_ = take_small()
            tpose(ps[bT_][:, 0:8], lgf[:, gs], i8s[:, :],
                  waits=list(wT_) + [("a", a_rt)])
            tp_pe.append(S.cur("pe"))
            tp_banks.append(bT_)

        if t + 1 < NT:
            pad_next = prologue_a_dma(t + 1, io_extra=[("v", phi_c["v_phi"])])
        else:
            pad_next = None

        bw_dw = {i: S.cur("bw%d" % i) for i in range(3)}
        base_banks = {}
        N_OPEN = 5

        def base_group(m, stop):
            bank, bw_ = take_big()
            for h in (0, 1):
                idx = 2 * m + h
                par = (t * 16 + idx) % 3
                waits = [("bw" + str(par), bw_dw[par])]
                if h == 0:
                    waits += list(bw_)
                if idx == 0:
                    waits.append(("v", v_xn))
                    waits.append(("g", g_xn))
                for kk in range(4):
                    k = h * 4 + kk
                    mm(ps[bank][:, 0:T], bwb[par][:, kk*128:(kk+1)*128],
                       uxn_t[:, k*T:(k+1)*T], k == 0, False,
                       waits=waits if kk == 0 else ())
                last_use[("bw", par)] = ("pe", S.cur("pe"))
                nidx = idx + 3
                npp = (t * 16 + nidx) % 3
                if nidx < 16:
                    nm_, nh = nidx // 2, nidx % 2
                    bw_dw[npp] = stream_dma(
                        "bw", npp, bwb[npp][:, :],
                        d_bw[nm_, :, nh*512:(nh+1)*512])
                elif t + 1 < NT and nidx < 19:
                    nh = nidx - 16
                    bw_dw[npp] = stream_dma(
                        "bw", npp, bwb[npp][:, :],
                        d_bw[nh // 2, :, (nh % 2)*512:(nh % 2 + 1)*512])
            return bank

        for m in range(N_OPEN):
            base_banks[m] = base_group(m, False)

        v_phi = phi_c["v_phi"]; w4w = phi_c["w4w"]
        a_sin = phi_c["a_sin"]; a_cos = phi_c["a_cos"]
        vA = vop(lambda e, u=uxn_t: e.tensor_tensor(ua[:, :], u[:, :], ukf[:, :],
                                                    op=OP.mult),
                 waits=[("a", a_cos), ("g", g_xn), ("v", phi_c["v_done"])])
        vB = vop(lambda e, u=uxn_t: e.tensor_tensor(ub[:, :], u[:, :], umr[:, :],
                                                    op=OP.mult), waits=[("a", a_sin)])
        v11 = vop(lambda e: e.tensor_tensor(uqo[:, :], uqo[:, :], uphi[:, :],
                                            op=OP.add),
                  waits=[("v", w4w), ("a", a_qoff)])
        e1 = vop(lambda e: e.tensor_scalar(ut1[:, :], uqo[:, :], math.pi, None,
                                           OP.is_gt),
                 waits=[("v", v11), ("v", vA), ("pe", pe_sumsq)])
        e2 = vop(lambda e: e.tensor_scalar(uphi[:, :], uqo[:, :], -math.pi, None,
                                           OP.is_lt), waits=[("v", v11)])
        e3 = vop(lambda e: e.tensor_tensor(uphi[:, :], uphi[:, :], ut1[:, :],
                                           op=OP.subtract), waits=[("v", e2)])
        e4 = vop(lambda e: e.scalar_tensor_tensor(ut1[:, :], uphi[:, :], TWO_PI,
                                                  uqo[:, :], op0=OP.mult, op1=OP.add),
                 waits=[("v", e3)])
        e5 = vop(lambda e: e.tensor_scalar(uphi[:, :], ut1[:, :], math.pi/2, None,
                                           OP.is_gt), waits=[("v", e4)])
        e6 = vop(lambda e: e.scalar_tensor_tensor(ut2[:, :], uphi[:, :], -TWO_PI,
                                                  ut1[:, :], op0=OP.mult, op1=OP.add),
                 waits=[("v", e5), ("v", vB)])
        e7 = vop(lambda e: e.tensor_scalar(ut2[:, :], ut2[:, :], math.pi/2, None,
                                           OP.add), waits=[("v", e6)])
        e8 = vop(lambda e: e.tensor_scalar(uphi[:, :], ut1[:, :], CLAMP, -CLAMP,
                                           OP.min, OP.max), waits=[("v", e6)])
        e9 = vop(lambda e: e.tensor_scalar(ut2[:, :], ut2[:, :], CLAMP, -CLAMP,
                                           OP.min, OP.max), waits=[("v", e7)])
        a_sq2 = act(uphi[:, :], uphi[:, :], AF.Sin, waits=[("v", e8)])
        a_cq2 = act(uqo[:, :], ut2[:, :], AF.Sin, waits=[("v", e9)])

        v_ax = vop(lambda e, b=bAX: e.tensor_copy(ax_s[:, :], ps[b][:, 0:T]),
                   waits=[("pe", pe_ax)])
        bank_free[bAX] = [("v", v_ax)]
        we_parts = []
        for gi in range(GC):
            gs = slice(gi * 128, (gi + 1) * 128)
            g1 = vop(lambda e, b=tp_banks[gi]: e.tensor_copy(ltm[:, :],
                                                             ps[b][:, 0:8]),
                     waits=[("pe", tp_pe[gi])])
            bank_free[tp_banks[gi]] = [("v", g1)]
            g2 = vop(lambda e: e.max(srt[:, :], ltm[:, :]), waits=[("v", g1)])
            g3 = vop(lambda e: e.tensor_scalar(nm[:, :], srt[:, 0:1], -1.0, None,
                                               OP.mult), waits=[("v", g2)])
            g4 = vop(lambda e: e.tensor_scalar(msk[:, :], ltm[:, :], srt[:, 1:2],
                                               None, OP.is_ge), waits=[("v", g2)])
            g5 = act(expt[:, :], ltm[:, :], AF.Exp, bias=nm[:, 0:1],
                     waits=[("v", g3)])
            g6 = vop(lambda e: e.tensor_tensor(expt[:, :], expt[:, :], msk[:, :],
                                               op=OP.mult),
                     waits=[("a", g5), ("v", g4)])
            g7 = vop(lambda e: e.reduce_sum(den[:, :], expt[:, :], axis=AXX),
                     waits=[("v", g6)])
            g8 = vop(lambda e: e.reciprocal(wiv[:, :], den[:, :]), waits=[("v", g7)])
            g9 = vop(lambda e: e.tensor_scalar(wtm[:, :], expt[:, :], wiv[:, 0:1],
                                               2.0, OP.mult, OP.mult),
                     waits=[("v", g8)])
            bB_, wB_ = take_small()
            tpose(ps[bB_][0:8, 0:128], wtm[:, :], i128s[:, :],
                  waits=list(wB_) + [("v", g9)])
            pe_bt = S.cur("pe")
            vw = vop(lambda e, b=bB_, gs=gs: e.tensor_copy(we_s[:, gs],
                                                           ps[b][0:8, 0:128]),
                     waits=[("pe", pe_bt)])
            bank_free[bB_] = [("v", vw)]
            we_parts.append(vw)
        bRP, wRP = take_small()
        mm(ps[bRP][:, 0:T], reps[:, :], we_s[:, :], True, True,
           waits=list(wRP) + [("v", we_parts[-1])])
        pe_rep = S.cur("pe")
        v_wax = vop(lambda e, b=bRP: e.tensor_tensor(ax_s[:, :], ax_s[:, :],
                                                     ps[b][:, 0:T], op=OP.mult),
                    waits=[("pe", pe_rep), ("v", v_ax)])
        bank_free[bRP] = [("v", v_wax)]

        silu_ev = {}
        for m in range(N_OPEN):
            bank = base_banks[m]
            mm(ps[bank][:, 0:T], lb_s[:, m*128:(m+1)*128], ax_s[:, :], False, True,
               waits=[("v", v_wax)] if m == 0 else ())
            pe_g = S.cur("pe")
            ev = act(upr[:, m*T:(m+1)*T], ps[bank][:, 0:T], AF.Silu,
                     bias=bb_s[:, m:m+1], waits=[("pe", pe_g)])
            bank_free[bank] = [("a", ev)]
            silu_ev[m] = ev
        for m in range(N_OPEN, DC):
            bank = base_group(m, False)
            mm(ps[bank][:, 0:T], lb_s[:, m*128:(m+1)*128], ax_s[:, :], False, True,
               waits=[("v", v_wax)])
            pe_g = S.cur("pe")
            ev = act(upr[:, m*T:(m+1)*T], ps[bank][:, 0:T], AF.Silu,
                     bias=bb_s[:, m:m+1], waits=[("pe", pe_g)])
            bank_free[bank] = [("a", ev)]
            silu_ev[m] = ev
        a_g = S.cur("a")
        pe_base_end = S.cur("pe")
        if pad_next is not None:
            prologue_sq(pad_next, uge_hz=[("pe", pe_w2_prev)])

        vw1 = vop(lambda e: e.scalar_tensor_tensor(ut1[:, :], umag[:, :], 5.0,
                                                   ua[:, :], op0=OP.mult, op1=OP.mult),
                  waits=[("a", a_mag), ("v", e4)])
        vw2 = vop(lambda e: e.scalar_tensor_tensor(ut2[:, :], umag[:, :], 5.0,
                                                   ub[:, :], op0=OP.mult, op1=OP.mult),
                  waits=[("v", e9)])
        for j, (src_u, dst_u, crow) in enumerate(
                [(umag, uac, 16), (ut1, umr, 0), (ut2, umi, 8)]):
            for k in range(DC):
                ks = slice(k*T, (k+1)*T)
                cc = slice(crow + k, crow + k + 1)
                w = []
                if j == 0 and k == 0:
                    w = [("a", a_mag)]
                if j == 1 and k == 0:
                    w = [("v", vw2), ("pe", pe_sumsq)]
                vs = vop(lambda e, ks=ks, cc=cc, s=src_u, d=dst_u:
                         e.tensor_tensor_scan(d[:, ks], sc1[:, :], s[:, ks],
                                              carry[:, cc], OP.mult, OP.add),
                         waits=w)
                lc = slice(k*T + T - 1, k*T + T)
                vop(lambda e, lc=lc, cc=cc, d=dst_u:
                    e.tensor_copy(carry[:, cc], d[:, lc]),
                    waits=[("v", vs)])
            if j == 0:
                g_scan = S.cur("v")
        v_scan = S.cur("v")

        bS4, wS4 = take_small()
        bQ4, wQ4 = take_small()

        def ln2_stat_unit(i, u, rdy, prev, split_runs=False):
            a_s = act(ukf[:, :], u[:, :], AF.Square,
                      waits=([("pe", prev)] if prev
                             else [("pe", pe_sumsq), ("v", vB)]) + [rdy])
            runs = ([("s", range(DC)), ("q", range(DC))] if split_runs
                    else [("sq", range(DC))])
            for which, ks_ in runs:
                for k in ks_:
                    gk = i * DC + k
                    if which in ("s", "sq"):
                        w = [rdy] if k == 0 else []
                        if gk == 0:
                            w += list(wS4)
                        mm(ps[bS4][0:1, 0:T], ones[:, 0:1], u[:, k*T:(k+1)*T],
                           gk == 0, gk == 31, waits=w)
                    if which in ("q", "sq"):
                        w = [("a", a_s)] if k == 0 else []
                        if gk == 0:
                            w += list(wQ4)
                        mm(ps[bQ4][32:33, 0:T], ones[:, 0:1], ukf[:, k*T:(k+1)*T],
                           gk == 0, gk == 31, waits=w, tp=(0, 32))
            return S.cur("pe")

        pe_prev_sq = ln2_stat_unit(0, ua, ("v", vA), 0)
        pe_prev_sq = ln2_stat_unit(1, ub, ("v", vB), pe_prev_sq)

        if pad_next is not None:
            prologue_mms(pad_next)

        a_ln = act(uac[:, :], uac[:, :], AF.Ln, bias=1e-8, scale=5.0,
                   waits=[("v", g_scan)])
        a_isq = act(uac[:, :], uac[:, :], AF.Exp, scale=-0.5,
                    waits=[("a", a_ln)])
        v20 = vop(lambda e: e.tensor_tensor(umr[:, :], umr[:, :], uac[:, :],
                                            op=OP.mult),
                  waits=[("a", a_isq), ("v", v_scan)])
        v21 = vop(lambda e: e.tensor_tensor(umi[:, :], umi[:, :], uac[:, :],
                                            op=OP.mult),
                  waits=[("a", a_isq), ("v", v_scan)])

        r1 = vop(lambda e: e.tensor_tensor(ut1[:, :], umr[:, :], uqo[:, :],
                                           op=OP.mult),
                 waits=[("v", v20), ("a", a_cq2)])
        r2 = vop(lambda e: e.tensor_tensor(ut2[:, :], umi[:, :], uphi[:, :],
                                           op=OP.mult),
                 waits=[("v", v21), ("a", a_sq2), ("a", a_cq2)])
        r3 = vop(lambda e: e.tensor_tensor(uac[:, :], ut1[:, :], ut2[:, :],
                                           op=OP.add), waits=[("v", r2)])
        r4 = vop(lambda e: e.tensor_tensor(ut1[:, :], umi[:, :], uqo[:, :],
                                           op=OP.mult), waits=[("v", r3)])
        r5 = vop(lambda e: e.tensor_tensor(ut2[:, :], umr[:, :], uphi[:, :],
                                           op=OP.mult), waits=[("v", r3)])
        r6 = vop(lambda e: e.tensor_tensor(umag[:, :], ut1[:, :], ut2[:, :],
                                           op=OP.subtract),
                 waits=[("v", r5), ("v", g_scan)])
        ctx_units = [ua, ub, uac, umag]

        pe_prev_sq = ln2_stat_unit(2, uac, ("v", r3), pe_prev_sq,
                                   split_runs=True)
        pe_prev_sq = ln2_stat_unit(3, umag, ("v", r6), pe_prev_sq,
                                   split_runs=True)
        pe_stats4 = S.cur("pe")
        w1_ = vop(lambda e, b=bS4: e.tensor_scalar(stat[0:1, R_MU], ps[b][0:1, 0:T],
                                                   1.0/4096.0, None, OP.mult),
                  waits=[("pe", pe_stats4)])
        w2_ = vop(lambda e: e.tensor_tensor(stat[0:1, R_TMP], stat[0:1, R_MU],
                                            stat[0:1, R_MU], op=OP.mult),
                  waits=[("v", w1_)])
        w3_ = vop(lambda e, b=bQ4: e.scalar_tensor_tensor(stat[0:1, R_VAR],
                                                          ps[b][32:33, 0:T],
                                                          1.0/4096.0, stat[0:1, R_TMP],
                                                          op0=OP.mult, op1=OP.subtract),
                  waits=[("v", w2_)])
        bank_free[bS4] = [("v", w1_)]
        bank_free[bQ4] = [("v", w3_)]
        a2 = act(stat[0:1, R_SD], stat[0:1, R_VAR], AF.Ln, bias=1e-5,
                 waits=[("v", w3_)])
        a2b = act(stat[0:1, R_ISTD], stat[0:1, R_SD], AF.Exp, scale=-0.5,
                  waits=[("a", a2)])
        bM4, wM4 = take_small()
        mm(ps[bM4][:, 0:T], onesr[0:1, 0:128], stat[0:1, R_MU], True, True,
           waits=list(wM4) + [("v", w2_)])
        pe_bc4m = S.cur("pe")
        bI4, wI4 = take_small()
        mm(ps[bI4][:, 0:T], onesr[0:1, 0:128], stat[0:1, R_ISTD], True, True,
           waits=list(wI4) + [("a", a2b)])
        pe_bc4i = S.cur("pe")
        wc1 = vop(lambda e, b=bM4: e.tensor_copy(mu4b[:, :], ps[b][:, 0:T]),
                  waits=[("pe", pe_bc4m)])
        wc2 = vop(lambda e, b=bI4: e.tensor_copy(is4b[:, :], ps[b][:, 0:T]),
                  waits=[("pe", pe_bc4i)])
        bank_free[bM4] = [("v", wc1)]
        bank_free[bI4] = [("v", wc2)]
        split_done = {}
        wv = wg = [("v", wc2)]
        for i, u in enumerate(ctx_units):
            for k in range(DC):
                ks = slice(k*T, (k+1)*T)
                c = i * DC + k
                eng = "vector" if k < 6 else "gpsimd"
                sg = "v" if k < 6 else "g"
                if k < 6:
                    w, wv = wv, []
                else:
                    w, wg = wg, []
                c0 = S.em(eng, lambda e, u=u, ks=ks: e.tensor_tensor(
                    u[:, ks], u[:, ks], mu4b[:, :], op=OP.subtract), waits=w)
                c1x = S.em(eng, lambda e, u=u, ks=ks: e.tensor_tensor(
                    u[:, ks], u[:, ks], is4b[:, :], op=OP.mult),
                    waits=[(sg, c0)])
                cnt = act(u[:, ks], u[:, ks], AF.Identity,
                          bias=lbt_s[:, c:c+1], scale=lg_s[:, c:c+1],
                          waits=[(sg, c1x)])
                split_done[c] = ("a", cnt)
        v_ctx = S.cur("v"); g_ctx = S.cur("g"); a_ctx = S.cur("a")

        w1_dw = {i: S.cur("w1%d" % i) for i in range(5)}
        for m in range(16):
            bank, bw_ = take_big()
            for qt in range(4):
                idx = 4 * m + qt
                par = (t * 64 + idx) % 5
                for kk in range(8):
                    k = qt * 8 + kk
                    u = ctx_units[k // DC]
                    ks = slice((k % DC) * T, (k % DC + 1) * T)
                    waits = []
                    if m == 0:
                        waits.append(split_done[k])
                    if kk == 0:
                        waits.append(("w1" + str(par), w1_dw[par]))
                        if qt == 0:
                            waits += list(bw_)
                        if idx == 4:
                            waits += [("v", v_ctx), ("g", g_ctx),
                                      ("a", a_ctx)]
                    mm(ps[bank][:, 0:T], w1b[par][:, kk*128:(kk+1)*128],
                       u[:, ks], k == 0, k == 31, waits=waits)
                pe_g = S.cur("pe")
                last_use[("w1", par)] = ("pe", pe_g)
                nidx = idx + 5
                npar = (t * 64 + nidx) % 5
                if nidx < 64:
                    nm_, nq = nidx // 4, nidx % 4
                    w1_dw[npar] = stream_dma(
                        "w1", npar, w1b[npar][:, :],
                        d_w1[nm_, :, nq*1024:(nq+1)*1024])
                elif t + 1 < NT and nidx < 69:
                    nq = nidx - 64
                    w1_dw[npar] = stream_dma(
                        "w1", npar, w1b[npar][:, :],
                        d_w1[nq // 4, :, (nq % 4)*1024:(nq % 4 + 1)*1024])
            ev = act(uge[:, m*T:(m+1)*T], ps[bank][:, 0:T],
                     AF.Gelu, bias=b1_s[:, m:m+1], waits=[("pe", S.cur("pe"))])
            bank_free[bank] = [("a", ev)]
        a_gelu = S.cur("a")

        if pad_next is not None:
            pro_next = prologue_b2(pad_next,
                                   row_extra=[("pe", pe_bc4i), ("a", a2b)])
            xn_next = emit_xn(pro_next, xn_extra=xn_hazard[(t + 1) % 2])
            phi_next = emit_phi(pro_next["pe_sumsq"],
                                waits0=[("g", pro_next["g_iota"]), ("v", r5)],
                                kf_extra=[("pe", pe_prev_sq)])
        else:
            pro_next = None
            xn_next = None
            phi_next = None

        w2_dw = {i: S.cur("w2%d" % i) for i in range(3)}
        for m in range(8):
            bank, bw_ = take_big()
            for q in range(4):
                idx = 4 * m + q
                par = (t * 32 + idx) % 3
                for kk in range(4):
                    k = q * 4 + kk
                    waits = []
                    if kk == 0:
                        waits.append(("w2" + str(par), w2_dw[par]))
                        if q == 0:
                            waits += list(bw_)
                        if idx == 0:
                            waits.append(("a", a_gelu))
                    mm(ps[bank][:, 0:T], w2b[par][:, kk*128:(kk+1)*128],
                       uge[:, k*T:(k+1)*T], k == 0, k == 15, waits=waits)
                pe_g = S.cur("pe")
                last_use[("w2", par)] = ("pe", pe_g)
                nidx = idx + 3
                npp = (t * 32 + nidx) % 3
                if nidx < 32:
                    nm_, nq = nidx // 4, nidx % 4
                    w2_dw[npp] = stream_dma(
                        "w2", npp, w2b[npp][:, :],
                        d_w2[nm_, :, nq*512:(nq+1)*512])
                elif t + 1 < NT and nidx < 35:
                    nq = nidx - 32
                    w2_dw[npp] = stream_dma(
                        "w2", npp, w2b[npp][:, :],
                        d_w2[0, :, nq*512:(nq+1)*512])
            ev = act(uqo[:, m*T:(m+1)*T], ps[bank][:, 0:T],
                     AF.Identity, bias=b2_s[:, m:m+1], waits=[("pe", S.cur("pe"))])
            bank_free[bank] = [("a", ev)]
        a_w2 = S.cur("a")
        pe_w2_prev = S.cur("pe")
        for k in range(DC):
            ks = slice(k*T, (k+1)*T)
            eng = "vector" if k < 6 else "gpsimd"
            w = [("a", a_w2)] if k in (0, 6) else []
            S.em(eng, lambda e, ks=ks, u=uxn_t: e.tensor_tensor(
                uqo[:, ks], uqo[:, ks], u[:, ks], op=OP.add), waits=w)
        vs_v = S.cur("v"); vs_g = S.cur("g")
        for k in range(DC):
            ks = slice(k*T, (k+1)*T)
            eng = "vector" if k < 6 else "gpsimd"
            w = ([("a", a_g), ("v", vs_v)] if k == 0 else
                 ([("a", a_g), ("g", vs_g)] if k == 6 else []))
            S.em(eng, lambda e, ks=ks: e.tensor_tensor(
                ut1[:, ks], uqo[:, ks], upr[:, ks], op=OP.mult), waits=w)
        f1_v = S.cur("v"); f1_g = S.cur("g")
        for k in range(DC):
            ks = slice(k*T, (k+1)*T)
            eng = "vector" if k < 6 else "gpsimd"
            w = ([("v", f1_v)] if k == 0 else
                 ([("g", f1_g)] if k == 6 else []))
            S.em(eng, lambda e, ks=ks, uxp=uxp: e.tensor_tensor(
                uxp[:, ks], uxp[:, ks], ut1[:, ks], op=OP.add), waits=w)
        f2_v = S.cur("v"); f2_g = S.cur("g")

        osrc = bass.AP(uxp, 0, [[U, 128], [T, 8], [1, T]])
        odst = bass.AP(d_out, t0, [[2048, 128], [128 * 2048, 8], [1, T]])
        gdma(odst, osrc, waits=[("v", f2_v), ("g", f2_g)], sem="o" + str(t % 2))
        xn_hazard[t % 2] = [("v", vs_v), ("g", vs_g), ("pe", pe_base_end)]
        S.barrier(engines=("vector", "scalar", "gpsimd"))
        pro = pro_next
        xn_c = xn_next
        phi_c = phi_next

    S.em("gpsimd", lambda e: e.engine_nop(),
         waits=[("o0", 16 * (NT // 2)), ("o1", 16 * (NT // 2))])

    with nc.Block() as block:
        def mk(name):
            def run(eng):
                for (e, waits, fn, sem, amt) in S.items:
                    if e != name:
                        continue
                    for sname, val in waits:
                        eng.wait_ge(sems[sname], val)
                    ins = fn(eng)
                    ins.then_inc(sems[sem], amt)
            return run
        block.sync(mk("sync"))
        block.vector(mk("vector"))
        block.scalar(mk("scalar"))
        block.tensor(mk("tensor"))
        block.gpsimd(mk("gpsimd"))

    es.close()
    return nc


def _stage_lhsT(w, K, M):
    return np.ascontiguousarray(
        w.reshape(M // 128, 128, K // 128, 128).transpose(0, 3, 2, 1))


def kernel(L0_attn_x, L0_attn_random_phases, L0_attn_W_mag, L0_attn_b_mag,
           L0_attn_W_q, L0_attn_b_q, L0_attn_ln_g, L0_attn_ln_b,
           L0_attn_W1, L0_attn_b1, L0_attn_W2, L0_attn_b2,
           L0_ffn_base_w, L0_ffn_base_b, L0_ffn_lora_A, L0_ffn_lora_B,
           L0_ffn_router_w, L0_ffn_router_b):
    f = np.asarray
    x = f(L0_attn_x, dtype=np.float32)

    rt = f(L0_ffn_router_w, dtype=np.float32).T
    rws = np.zeros((128, 64), dtype=np.float32)
    for k in range(8):
        rws[:, k*8:(k+1)*8] = rt[k*128:(k+1)*128, :]

    common = {
        "wmag": _stage_lhsT(f(L0_attn_W_mag, dtype=np.float32), 1024, 1024)
                .reshape(8, 128, 1024),
        "wqm": _stage_lhsT(f(L0_attn_W_q, dtype=np.float32), 1024, 1024)
               .reshape(8, 128, 1024),
        "bwm": _stage_lhsT(f(L0_ffn_base_w, dtype=np.float32), 1024, 1024)
               .reshape(8, 128, 1024),
        "w1m": _stage_lhsT(f(L0_attn_W1, dtype=np.float32), 4096, 2048)
               .reshape(16, 128, 4096),
        "w2m": _stage_lhsT(f(L0_attn_W2, dtype=np.float32), 2048, 1024)
               .reshape(8, 128, 2048),
        "lam": np.ascontiguousarray(
            f(L0_ffn_lora_A, dtype=np.float32).transpose(1, 0, 2).reshape(1024, 128)
            .reshape(8, 128, 128).transpose(1, 0, 2).reshape(128, 1024)),
        "lbm": np.ascontiguousarray(
            f(L0_ffn_lora_B, dtype=np.float32).reshape(128, 1024)),
        "rwm": rws,
        "bmag": np.ascontiguousarray(f(L0_attn_b_mag, np.float32).reshape(8, 128).T),
        "bq": np.ascontiguousarray(f(L0_attn_b_q, np.float32).reshape(8, 128).T),
        "b1": np.ascontiguousarray(f(L0_attn_b1, np.float32).reshape(16, 128).T),
        "b2": np.ascontiguousarray(f(L0_attn_b2, np.float32).reshape(8, 128).T),
        "bb": np.ascontiguousarray(f(L0_ffn_base_b, np.float32).reshape(8, 128).T),
        "lg": np.ascontiguousarray(f(L0_attn_ln_g, np.float32).reshape(32, 128).T),
        "lbt": np.ascontiguousarray(f(L0_attn_ln_b, np.float32).reshape(32, 128).T),
        "rb": np.ascontiguousarray(f(L0_ffn_router_b, np.float32).reshape(8, 1)),
        "ph": np.ascontiguousarray(
            f(L0_attn_random_phases, np.float32).reshape(8, 128).T),
        "i128": np.eye(128, dtype=np.float32),
        "i8": np.eye(8, dtype=np.float32),
        "rep": np.ascontiguousarray(
            (np.arange(128)[None, :] // 16 == np.arange(8)[:, None])
            .astype(np.float32)),
    }

    in_maps = []
    for c in range(8):
        m = dict(common)
        m["x"] = np.ascontiguousarray(x[c].T)
        in_maps.append(m)

    nc = build_program()
    res = run_bass_kernel_spmd(nc, in_maps, core_ids=list(range(8)))
    out = np.stack([np.ascontiguousarray(r["o"].T) for r in res.results], axis=0)
    return out.astype(np.float32)


# version 41
# speedup vs baseline: 1.0117x; 1.0117x over previous
import math
import contextlib
import numpy as np
import concourse.bass as bass
import concourse.mybir as mybir
from concourse.bass_utils import run_bass_kernel_spmd

F32 = mybir.dt.float32
BF16 = mybir.dt.bfloat16
I32 = mybir.dt.int32
AF = mybir.ActivationFunctionType
OP = mybir.AluOpType
AXX = mybir.AxisListType.X

T = 256
NT = 2048 // T
DC = 8
GC = T // 128

TWO_PI = 2.0 * math.pi
B_OFF = 256.0
CLAMP = float(np.nextafter(np.float32(np.pi), np.float32(0.0)))


def _cody_consts():
    u = np.float32(TWO_PI).view(np.uint32)
    c1 = np.uint32(u & np.uint32(0xFFFFF000)).view(np.float32)
    r = TWO_PI - float(c1)
    u2 = np.float32(r).view(np.uint32)
    c2 = np.uint32(u2 & np.uint32(0xFFFFF000)).view(np.float32)
    c3 = TWO_PI - float(c1) - float(c2)
    return float(c1), float(c2), float(c3)


C1, C2, C3 = _cody_consts()


class _Sched:
    ENG_SEM = {"tensor": "pe", "vector": "v", "scalar": "a",
               "gpsimd": "g", "sync": "s"}

    def __init__(self):
        self.items = []
        self.cnt = {}
        self.pending = {}

    def cur(self, sem):
        return self.cnt.get(sem, 0)

    def em(self, eng, fn, waits=(), sem=None, amt=1):
        if sem is None:
            sem = self.ENG_SEM[eng]
        w = dict(self.pending.pop(eng, {}))
        for s, v in waits:
            if v > 0:
                w[s] = max(w.get(s, 0), v)
        self.cnt[sem] = self.cnt.get(sem, 0) + amt
        self.items.append((eng, tuple(w.items()), fn, sem, amt))
        return self.cnt[sem]

    def barrier(self, engines=("tensor", "vector", "scalar", "gpsimd")):
        snap = {s: self.cnt.get(s, 0)
                for s in ("pe", "v", "a", "g") if self.cnt.get(s, 0) > 0}
        for e in engines:
            self.pending[e] = dict(snap)


def build_program():
    nc = bass.Bass("TRN2", target_bir_lowering=False)

    d_x = nc.dram_tensor("x", [1024, 2048], F32, kind="ExternalInput")
    d_wmag = nc.dram_tensor("wmag", [8, 128, 1024], F32, kind="ExternalInput")
    d_wq = nc.dram_tensor("wqm", [8, 128, 1024], F32, kind="ExternalInput")
    d_bw = nc.dram_tensor("bwm", [8, 128, 1024], F32, kind="ExternalInput")
    d_w1 = nc.dram_tensor("w1m", [16, 128, 4096], F32, kind="ExternalInput")
    d_w2 = nc.dram_tensor("w2m", [8, 128, 2048], F32, kind="ExternalInput")
    d_la = nc.dram_tensor("lam", [128, 1024], F32, kind="ExternalInput")
    d_lb = nc.dram_tensor("lbm", [128, 1024], F32, kind="ExternalInput")
    d_rw = nc.dram_tensor("rwm", [128, 64], F32, kind="ExternalInput")
    d_bmag = nc.dram_tensor("bmag", [128, 8], F32, kind="ExternalInput")
    d_bq = nc.dram_tensor("bq", [128, 8], F32, kind="ExternalInput")
    d_b1 = nc.dram_tensor("b1", [128, 16], F32, kind="ExternalInput")
    d_b2 = nc.dram_tensor("b2", [128, 8], F32, kind="ExternalInput")
    d_bb = nc.dram_tensor("bb", [128, 8], F32, kind="ExternalInput")
    d_lg = nc.dram_tensor("lg", [128, 32], F32, kind="ExternalInput")
    d_lbt = nc.dram_tensor("lbt", [128, 32], F32, kind="ExternalInput")
    d_rb = nc.dram_tensor("rb", [8, 1], F32, kind="ExternalInput")
    d_ph = nc.dram_tensor("ph", [128, 8], F32, kind="ExternalInput")
    d_i128 = nc.dram_tensor("i128", [128, 128], F32, kind="ExternalInput")
    d_i8 = nc.dram_tensor("i8", [8, 8], F32, kind="ExternalInput")
    d_rep = nc.dram_tensor("rep", [8, 128], F32, kind="ExternalInput")
    d_out = nc.dram_tensor("o", [1024, 2048], F32, kind="ExternalOutput")

    es = contextlib.ExitStack()
    sb = lambda name, shape, dt=F32: es.enter_context(nc.sbuf_tensor(name, shape, dt))
    psb = lambda name: es.enter_context(nc.psum_tensor(name, [128, 512], F32))

    U = 2048
    ux = sb("ux", [128, U]); ux2 = sb("ux2", [128, U]); uxn = sb("uxn", [128, U])
    uxn2 = sb("uxn2", [128, U])
    umag = sb("umag", [128, U]); uqo = sb("uqo", [128, U]); uphi = sb("uphi", [128, U])
    ukf = sb("ukf", [128, U]); upr = sb("upr", [128, U])
    ua = sb("ua", [128, U]); ub = sb("ub", [128, U])
    umr = sb("umr", [128, U]); umi = sb("umi", [128, U]); uac = sb("uac", [128, U])
    ut1 = sb("ut1", [128, U]); ut2 = sb("ut2", [128, U])
    uki = sb("uki", [128, 128], I32)
    uge = sb("uge", [128, 16 * T])
    iota = sb("iota", [128, T])
    sc1 = sb("sc1", [128, T])
    wmb = [sb("wmb%d" % i, [128, 512]) for i in range(3)]
    wqb = [sb("wqb%d" % i, [128, 512]) for i in range(3)]
    bwb = [sb("bwb%d" % i, [128, 512]) for i in range(3)]
    w1b = [sb("w1b%d" % i, [128, 1024]) for i in range(5)]
    w2b = [sb("w2b%d" % i, [128, 512]) for i in range(3)]
    la_s = sb("la_s", [128, 1024]); lb_s = sb("lb_s", [128, 1024])
    rw_s = sb("rw_s", [128, 64])
    stat = sb("stat", [1, 768])
    carry = sb("carry", [128, 24])
    mu4b = sb("mu4b", [128, T]); is4b = sb("is4b", [128, T])
    i128s = sb("i128s", [128, 128]); i8s = sb("i8s", [8, 8]); reps = sb("reps", [8, 128])
    bmag_s = sb("bmag_s", [128, 8]); bq_s = sb("bq_s", [128, 8])
    b1_s = sb("b1_s", [128, 16]); b2_s = sb("b2_s", [128, 8]); bb_s = sb("bb_s", [128, 8])
    lg_s = sb("lg_s", [128, 32]); lbt_s = sb("lbt_s", [128, 32])
    rb_s = sb("rb_s", [8, 1]); ph_s = sb("ph_s", [128, 8]); phm = sb("phm", [128, 8])
    lgf = sb("lgf", [8, T]); ax_s = sb("ax_s", [128, T])
    we_s = lgf
    ltm = sb("ltm", [128, 8]); srt = sb("srt", [128, 8]); msk = sb("msk", [128, 8])
    expt = sb("expt", [128, 8]); wtm = sb("wtm", [128, 8])
    nm = sb("nm", [128, 1]); den = sb("den", [128, 1]); wiv = sb("wiv", [128, 1])
    ones = sb("ones", [128, 1]); onesr = sc1
    cbias = sb("cbias", [128, 4])

    ps = [psb("ps%d" % i) for i in range(8)]

    sem_names = ["pe", "v", "a", "g", "s", "o0", "o1", "xq0", "xq1",
                 "wm0", "wm1", "wm2", "wq0", "wq1", "wq2", "bw0", "bw1", "bw2",
                 "w10", "w11", "w12", "w13", "w14", "w15", "w16", "w17", "w20", "w21", "w22", "w23", "cst"]
    sems = {n: es.enter_context(nc.semaphore(n)) for n in sem_names}

    S = _Sched()

    def dma(q_sem, dst_ap, src_ap, waits=()):
        return S.em("sync", lambda e, d=dst_ap, s=src_ap: e.dma_start(d, s),
                    waits=waits, sem=q_sem, amt=16)

    def gdma(dst_ap, src_ap, waits=(), sem="o0"):
        return S.em("gpsimd", lambda e, d=dst_ap, s=src_ap: e.dma_start(d, s),
                    waits=waits, sem=sem, amt=16)

    def mm(out_ap, lhs_ap, rhs_ap, start, stop, waits=(), tp=None):
        return S.em("tensor",
                    lambda e, o=out_ap, l=lhs_ap, r=rhs_ap, st=start, sp=stop, tp=tp:
                    e.matmul(o, l, r, start=st, stop=sp, tile_position=tp),
                    waits=waits)

    def tpose(out_ap, in_ap, ident_ap, waits=()):
        return S.em("tensor",
                    lambda e, o=out_ap, i=in_ap, d=ident_ap:
                    e.transpose(o, i, d), waits=waits)

    def act(out_ap, in_ap, func, bias=0.0, scale=1.0, waits=()):
        return S.em("scalar",
                    lambda e, o=out_ap, i=in_ap, f=func, b=bias, sc=scale:
                    e.activation(o, i, f, bias=b, scale=sc), waits=waits)

    def vop(fn, waits=()):
        return S.em("vector", fn, waits=waits)

    R_MU, R_TMP, R_VAR = slice(0, T), slice(T, 2*T), slice(2*T, 3*T)
    R_SD, R_ISTD = R_TMP, R_VAR

    bank_free = {b: [] for b in range(8)}
    BIGS = [0, 1, 2, 3, 4]
    SMALLS = [5, 6, 7]
    rot = {"big": [0], "small": [0]}

    def _take(pool, key):
        r = rot[key]
        for i in range(len(pool)):
            b = pool[(r[0] + i) % len(pool)]
            if bank_free[b] is not None:
                r[0] = (r[0] + i + 1) % len(pool)
                w = bank_free[b]
                bank_free[b] = None
                return b, w
        raise RuntimeError(f"no released {key} PSUM bank available")

    def take_big():
        return _take(BIGS, "big")

    def take_small():
        return _take(SMALLS, "small")

    cst_target = 0
    const_pairs = [
        (la_s[:, :], d_la[:, :]), (lb_s[:, :], d_lb[:, :]), (rw_s[:, :], d_rw[:, :]),
        (bmag_s[:, :], d_bmag[:, :]), (bq_s[:, :], d_bq[:, :]),
        (b1_s[:, :], d_b1[:, :]), (b2_s[:, :], d_b2[:, :]), (bb_s[:, :], d_bb[:, :]),
        (lg_s[:, :], d_lg[:, :]), (lbt_s[:, :], d_lbt[:, :]),
        (rb_s[:, :], d_rb[:, :]), (ph_s[:, :], d_ph[:, :]),
        (i128s[:, :], d_i128[:, :]), (i8s[:, :], d_i8[:, :]), (reps[:, :], d_rep[:, :]),
    ]
    for dst, src in const_pairs:
        cst_target = dma("cst", dst, src)

    for i, val in enumerate([1e-5, 1e-8, 5.0, 0.0]):
        S.em("vector", lambda e, i=i, v=val: e.memset(cbias[:, i:i+1], v))
        nc.const_aps.register(F32, val, cbias[:, i:i+1])
    S.em("vector", lambda e: e.memset(sc1[:, :], 1.0))
    S.em("vector", lambda e: e.memset(ones[:, :], 1.0))
    S.em("vector", lambda e: e.memset(carry[:, :], 0.0))
    S.em("vector", lambda e: e.tensor_scalar(phm[:, :], ph_s[:, :], 0.01, None, OP.mult),
         waits=[("cst", cst_target)])
    S.barrier()

    last_use = {}
    S_DEPTH = {"w1": 5, "w2": 3, "wm": 3, "wq": 3, "bw": 3}

    def stream_dma(stream, idx, dst, src):
        par = idx % S_DEPTH[stream]
        q = stream + str(par)
        waits = []
        lu = last_use.get((stream, par))
        if lu:
            waits.append(lu)
        return dma(q, dst, src, waits=waits)

    def prologue_a_dma(j, io_extra=()):
        t0j = j * T
        uxj = ux if j % 2 == 0 else ux2
        xw = [("o" + str(j % 2), 16 * ((j - j % 2) // 2))] if j > 1 else []
        srcap = bass.AP(d_x, t0j, [[2048, 128], [128 * 2048, 8], [1, T]])
        dstap = bass.AP(uxj, 0, [[U, 128], [T, 8], [1, T]])
        xqs = "xq%d" % (j % 2)
        x_dma = dma(xqs, dstap, srcap, waits=xw)
        g_iota = S.em("gpsimd", lambda e, t0j=t0j: e.iota(
            iota[:, :], [[1, T]], base=t0j, channel_multiplier=0,
            allow_small_or_imprecise_dtypes=True), waits=list(io_extra))
        if j == 0:
            for h in range(3):
                stream_dma("wm", h, wmb[h][:, :],
                           d_wmag[h // 2, :, (h % 2)*512:(h % 2 + 1)*512])
                stream_dma("wq", h, wqb[h][:, :],
                           d_wq[h // 2, :, (h % 2)*512:(h % 2 + 1)*512])
        if j == 0:
            for h in range(5):
                stream_dma("w1", h, w1b[h][:, :],
                           d_w1[h // 4, :, (h % 4)*1024:(h % 4 + 1)*1024])
            for h in range(3):
                stream_dma("w2", h, w2b[h][:, :], d_w2[0, :, h*512:(h+1)*512])
        if j == 0:
            for h in range(3):
                stream_dma("bw", h, bwb[h][:, :],
                           d_bw[h // 2, :, (h % 2)*512:(h % 2 + 1)*512])
        return {"x_dma": x_dma, "g_iota": g_iota, "uxj": uxj, "j": j,
                "xqs": xqs}

    def prologue_sq(pa, uge_hz=()):
        pa["a_sq"] = act(uge[:, 0:U], pa["uxj"][:, :], AF.Square,
                         waits=[(pa["xqs"], pa["x_dma"])] + list(uge_hz))

    def prologue_mms(pa):
        uxj = pa["uxj"]; x_dma = pa["x_dma"]
        bS, wS = take_small()
        for k in range(DC):
            w = list(wS) + [(pa["xqs"], x_dma)] if k == 0 else []
            mm(ps[bS][0:1, 0:T], ones[:, 0:1], uxj[:, k*T:(k+1)*T],
               k == 0, k == DC-1, waits=w)
        pa["bS"] = bS
        pa["pe_sum"] = S.cur("pe")
        bQ, wQ = take_big()
        for k in range(DC):
            w = list(wQ) + [("a", pa["a_sq"])] if k == 0 else []
            mm(ps[bQ][32:33, 0:T], ones[:, 0:1], uge[:, k*T:(k+1)*T],
               k == 0, k == DC-1, waits=w, tp=(0, 32))
        pa["bQ"] = bQ
        pa["pe_sumsq"] = S.cur("pe")

    def prologue_b2(pa, row_extra=()):
        bS = pa["bS"]; bQ = pa["bQ"]
        pe_sum = pa["pe_sum"]; pe_sumsq = pa["pe_sumsq"]
        v1 = vop(lambda e, b=bS: e.tensor_scalar(stat[0:1, R_MU], ps[b][0:1, 0:T],
                                                 1.0/1024.0, None, OP.mult),
                 waits=[("pe", pe_sum)] + list(row_extra))
        v2 = vop(lambda e: e.tensor_tensor(stat[0:1, R_TMP], stat[0:1, R_MU],
                                           stat[0:1, R_MU], op=OP.mult),
                 waits=[("v", v1)])
        v3 = vop(lambda e, b=bQ: e.scalar_tensor_tensor(stat[0:1, R_VAR],
                                                        ps[b][32:33, 0:T],
                                                        1.0/1024.0, stat[0:1, R_TMP],
                                                        op0=OP.mult, op1=OP.subtract),
                 waits=[("pe", pe_sumsq), ("v", v2)])
        bank_free[bS] = [("v", v1)]
        bank_free[bQ] = [("v", v3)]
        a1 = act(stat[0:1, R_SD], stat[0:1, R_VAR], AF.Ln, bias=1e-5,
                 waits=[("v", v3)])
        a1b = act(stat[0:1, R_ISTD], stat[0:1, R_SD], AF.Exp, scale=-0.5,
                  waits=[("a", a1)])
        bM, wM = take_small()
        mm(ps[bM][:, 0:T], onesr[0:1, 0:128], stat[0:1, R_MU], True, True,
           waits=list(wM) + [("v", v2)])
        pe_bcm = S.cur("pe")
        bI, wI = take_small()
        mm(ps[bI][:, 0:T], onesr[0:1, 0:128], stat[0:1, R_ISTD], True, True,
           waits=list(wI) + [("a", a1b)])
        pe_bci = S.cur("pe")
        return {"x_dma": pa["x_dma"], "g_iota": pa["g_iota"],
                "pe_sumsq": pe_sumsq,
                "pe_bcm": pe_bcm, "pe_bci": pe_bci, "bM": bM, "bI": bI,
                "uxj": pa["uxj"], "j": pa["j"]}

    def emit_xn(pro_j, xn_extra=()):
        uxj = pro_j["uxj"]
        bM = pro_j["bM"]; bI = pro_j["bI"]
        pe_bcm = pro_j["pe_bcm"]; pe_bci = pro_j["pe_bci"]
        dst = uxn if pro_j["j"] % 2 == 0 else uxn2
        for k in range(DC):
            ks = slice(k*T, (k+1)*T)
            w = []
            if k == 0:
                w = [("pe", pe_bcm)] + list(xn_extra)
            S.em("vector", lambda e, ks=ks, uxj=uxj, dst=dst, b=bM: e.tensor_tensor(
                dst[:, ks], uxj[:, ks], ps[b][:, 0:T], op=OP.subtract), waits=w)
        v_sub = S.cur("v")
        for k in range(DC):
            ks = slice(k*T, (k+1)*T)
            w = [("v", v_sub), ("pe", pe_bci)] if k == 0 else []
            S.em("vector", lambda e, ks=ks, dst=dst, b=bI: e.tensor_tensor(
                dst[:, ks], dst[:, ks], ps[b][:, 0:T], op=OP.mult), waits=w)
        v_xn_ = S.cur("v")
        bank_free[bM] = [("v", v_xn_)]
        bank_free[bI] = [("v", v_xn_)]
        return v_xn_, 0

    def emit_phi(pe_sumsq_j, waits0=(), kf_extra=()):
        for k in range(DC):
            ks = slice(k*T, (k+1)*T)
            w = list(waits0) if k == 0 else []
            vop(lambda e, ks=ks, k=k: e.tensor_scalar(
                uphi[:, ks], iota[:, :], phm[:, k:k+1], None, OP.mult), waits=w)
        v_phi = S.cur("v")
        v5 = vop(lambda e: e.tensor_scalar(umi[:, :], uphi[:, :], 1.0/TWO_PI,
                                           B_OFF + 0.5, OP.mult, OP.add),
                 waits=[("v", v_phi)])
        prev = v5
        for k in range(DC):
            for hh in (0, 1):
                ks = slice(k*T + hh*128, k*T + (hh+1)*128)
                v6k = vop(lambda e, ks=ks: e.tensor_copy(uki[:, :], umi[:, ks]),
                          waits=[("v", prev)])
                prev = vop(lambda e, ks=ks: e.tensor_copy(umi[:, ks], uki[:, :]),
                           waits=[("v", v6k)])
        v8 = vop(lambda e: e.tensor_scalar(umi[:, :], umi[:, :], -B_OFF, None,
                                           OP.add), waits=[("v", prev)])
        c1_ = vop(lambda e: e.scalar_tensor_tensor(ukf[:, :], umi[:, :], -C1,
                                                   uphi[:, :], op0=OP.mult,
                                                   op1=OP.add),
                  waits=[("v", v8)] + list(kf_extra))
        c2_ = vop(lambda e: e.scalar_tensor_tensor(uphi[:, :], umi[:, :], -C2,
                                                   ukf[:, :], op0=OP.mult,
                                                   op1=OP.add),
                  waits=[("v", c1_)])
        c3_ = vop(lambda e: e.scalar_tensor_tensor(ukf[:, :], umi[:, :], -C3,
                                                   uphi[:, :], op0=OP.mult,
                                                   op1=OP.add),
                  waits=[("v", c2_)])
        u1 = vop(lambda e: e.tensor_scalar(umi[:, :], ukf[:, :], -math.pi, None,
                                           OP.is_lt), waits=[("v", c3_)])
        u2 = vop(lambda e: e.scalar_tensor_tensor(umi[:, :], ukf[:, :], math.pi,
                                                  umi[:, :], op0=OP.is_gt,
                                                  op1=OP.subtract),
                 waits=[("v", u1)])
        w4w = vop(lambda e: e.scalar_tensor_tensor(uphi[:, :], umi[:, :], -TWO_PI,
                                                   ukf[:, :], op0=OP.mult,
                                                   op1=OP.add),
                  waits=[("v", u2)])
        d1 = vop(lambda e: e.tensor_scalar(umi[:, :], uphi[:, :], math.pi/2, None,
                                           OP.is_gt), waits=[("v", w4w)])
        d2 = vop(lambda e: e.scalar_tensor_tensor(ukf[:, :], umi[:, :], -TWO_PI,
                                                  uphi[:, :], op0=OP.mult,
                                                  op1=OP.add), waits=[("v", d1)])
        d3 = vop(lambda e: e.tensor_scalar(ukf[:, :], ukf[:, :], math.pi/2, None,
                                           OP.add), waits=[("v", d2)])
        v13 = vop(lambda e: e.tensor_scalar(umi[:, :], uphi[:, :], CLAMP, -CLAMP,
                                            OP.min, OP.max), waits=[("v", w4w)])
        v14 = vop(lambda e: e.tensor_scalar(ukf[:, :], ukf[:, :], CLAMP, -CLAMP,
                                            OP.min, OP.max), waits=[("v", d3)])
        a_sin = act(umr[:, :], umi[:, :], AF.Sin,
                    waits=[("v", v13), ("pe", pe_sumsq_j)])
        a_cos = act(ukf[:, :], ukf[:, :], AF.Sin, waits=[("v", v14)])
        return {"v_phi": v_phi, "w4w": w4w, "a_sin": a_sin, "a_cos": a_cos,
                "v_done": S.cur("v")}

    pro = None
    xn_c = None
    xn_hazard = {0: [], 1: []}
    pe_w2_prev = 0
    for t in range(NT):
        t0 = t * T
        uxp = ux if t % 2 == 0 else ux2
        uxn_t = uxn if t % 2 == 0 else uxn2
        if t == 0:
            pa0 = prologue_a_dma(0)
            prologue_sq(pa0)
            prologue_mms(pa0)
            pro = prologue_b2(pa0)
            xn_c = emit_xn(pro)
            phi_c = emit_phi(pro["pe_sumsq"],
                             waits0=[("g", pro["g_iota"]), ("cst", cst_target)])
        x_dma = pro["x_dma"]; g_iota = pro["g_iota"]; pe_sumsq = pro["pe_sumsq"]
        v_xn, g_xn = xn_c

        def proj_loop(dram, bufs, stream, out_unit, func, bias_sb, dma_w):
            evacs = []
            for m in range(DC):
                bank, bw_ = take_big()
                for h in (0, 1):
                    idx = 2 * m + h
                    par = (t * 16 + idx) % 3
                    waits = [(stream + str(par), dma_w[par])]
                    if h == 0:
                        waits += list(bw_)
                    if idx == 0:
                        waits.append(("v", v_xn))
                        waits.append(("g", g_xn))
                    for kk in range(4):
                        k = h * 4 + kk
                        mm(ps[bank][:, 0:T], bufs[par][:, kk*128:(kk+1)*128],
                           uxn_t[:, k*T:(k+1)*T], k == 0, k == DC-1,
                           waits=waits if kk == 0 else ())
                    pe_g = S.cur("pe")
                    last_use[(stream, par)] = ("pe", pe_g)
                    nidx = idx + 3
                    npp = (t * 16 + nidx) % 3
                    if nidx < 16:
                        nm_, nh = nidx // 2, nidx % 2
                        dma_w[npp] = stream_dma(
                            stream, npp, bufs[npp][:, :],
                            dram[nm_, :, nh*512:(nh+1)*512])
                    elif t + 1 < NT and nidx < 19:
                        nh = nidx - 16
                        dma_w[npp] = stream_dma(
                            stream, npp, bufs[npp][:, :],
                            dram[nh // 2, :, (nh % 2)*512:(nh % 2 + 1)*512])
                ev = act(out_unit[:, m*T:(m+1)*T], ps[bank][:, 0:T],
                         func, bias=bias_sb[:, m:m+1], waits=[("pe", S.cur("pe"))])
                bank_free[bank] = [("a", ev)]
                evacs.append(ev)
            return evacs

        mag_ev = proj_loop(d_wmag, wmb, "wm", umag, AF.Sigmoid, bmag_s,
                           {i: S.cur("wm%d" % i) for i in range(3)})
        qoff_ev = proj_loop(d_wq, wqb, "wq", uqo, AF.Identity, bq_s,
                            {i: S.cur("wq%d" % i) for i in range(3)})
        a_mag = mag_ev[-1]
        a_qoff = qoff_ev[-1]

        bAX, wAX = take_small()
        for k in range(DC):
            w = list(wAX) + [("v", v_xn), ("g", g_xn)] if k == 0 else []
            mm(ps[bAX][:, 0:T], la_s[:, k*128:(k+1)*128], uxn_t[:, k*T:(k+1)*T],
               k == 0, k == DC-1, waits=w)
        pe_ax = S.cur("pe")
        bRT, wRT = take_small()
        for k in range(DC):
            w = list(wRT) + [("v", v_xn), ("g", g_xn)] if k == 0 else []
            mm(ps[bRT][0:8, 0:T], rw_s[:, k*8:(k+1)*8], uxn_t[:, k*T:(k+1)*T],
               k == 0, k == DC-1, waits=w)
        pe_rt = S.cur("pe")
        a_rt = act(lgf[:, :], ps[bRT][0:8, 0:T], AF.Identity, bias=rb_s[0:8, 0:1],
                   waits=[("pe", pe_rt)])
        bank_free[bRT] = [("a", a_rt)]
        tp_pe = []
        tp_banks = []
        for gi in range(GC):
            gs = slice(gi * 128, (gi + 1) * 128)
            bT_, wT_ = take_small()
            tpose(ps[bT_][:, 0:8], lgf[:, gs], i8s[:, :],
                  waits=list(wT_) + [("a", a_rt)])
            tp_pe.append(S.cur("pe"))
            tp_banks.append(bT_)

        if t + 1 < NT:
            pad_next = prologue_a_dma(t + 1, io_extra=[("v", phi_c["v_phi"])])
        else:
            pad_next = None

        bw_dw = {i: S.cur("bw%d" % i) for i in range(3)}
        base_banks = {}
        N_OPEN = 5

        def base_group(m, stop):
            bank, bw_ = take_big()
            for h in (0, 1):
                idx = 2 * m + h
                par = (t * 16 + idx) % 3
                waits = [("bw" + str(par), bw_dw[par])]
                if h == 0:
                    waits += list(bw_)
                if idx == 0:
                    waits.append(("v", v_xn))
                    waits.append(("g", g_xn))
                for kk in range(4):
                    k = h * 4 + kk
                    mm(ps[bank][:, 0:T], bwb[par][:, kk*128:(kk+1)*128],
                       uxn_t[:, k*T:(k+1)*T], k == 0, False,
                       waits=waits if kk == 0 else ())
                last_use[("bw", par)] = ("pe", S.cur("pe"))
                nidx = idx + 3
                npp = (t * 16 + nidx) % 3
                if nidx < 16:
                    nm_, nh = nidx // 2, nidx % 2
                    bw_dw[npp] = stream_dma(
                        "bw", npp, bwb[npp][:, :],
                        d_bw[nm_, :, nh*512:(nh+1)*512])
                elif t + 1 < NT and nidx < 19:
                    nh = nidx - 16
                    bw_dw[npp] = stream_dma(
                        "bw", npp, bwb[npp][:, :],
                        d_bw[nh // 2, :, (nh % 2)*512:(nh % 2 + 1)*512])
            return bank

        for m in range(N_OPEN):
            base_banks[m] = base_group(m, False)

        v_phi = phi_c["v_phi"]; w4w = phi_c["w4w"]
        a_sin = phi_c["a_sin"]; a_cos = phi_c["a_cos"]
        vA = vop(lambda e, u=uxn_t: e.tensor_tensor(ua[:, :], u[:, :], ukf[:, :],
                                                    op=OP.mult),
                 waits=[("a", a_cos), ("g", g_xn), ("v", phi_c["v_done"])])
        vB = vop(lambda e, u=uxn_t: e.tensor_tensor(ub[:, :], u[:, :], umr[:, :],
                                                    op=OP.mult), waits=[("a", a_sin)])
        v11 = vop(lambda e: e.tensor_tensor(uqo[:, :], uqo[:, :], uphi[:, :],
                                            op=OP.add),
                  waits=[("v", w4w), ("a", a_qoff)])
        e1 = vop(lambda e: e.tensor_scalar(ut1[:, :], uqo[:, :], math.pi, None,
                                           OP.is_gt),
                 waits=[("v", v11), ("v", vA), ("pe", pe_sumsq)])
        e2 = vop(lambda e: e.tensor_scalar(uphi[:, :], uqo[:, :], -math.pi, None,
                                           OP.is_lt), waits=[("v", v11)])
        e3 = vop(lambda e: e.tensor_tensor(uphi[:, :], uphi[:, :], ut1[:, :],
                                           op=OP.subtract), waits=[("v", e2)])
        e4 = vop(lambda e: e.scalar_tensor_tensor(ut1[:, :], uphi[:, :], TWO_PI,
                                                  uqo[:, :], op0=OP.mult, op1=OP.add),
                 waits=[("v", e3)])
        e5 = vop(lambda e: e.tensor_scalar(uphi[:, :], ut1[:, :], math.pi/2, None,
                                           OP.is_gt), waits=[("v", e4)])
        e6 = vop(lambda e: e.scalar_tensor_tensor(ut2[:, :], uphi[:, :], -TWO_PI,
                                                  ut1[:, :], op0=OP.mult, op1=OP.add),
                 waits=[("v", e5), ("v", vB)])
        e7 = vop(lambda e: e.tensor_scalar(ut2[:, :], ut2[:, :], math.pi/2, None,
                                           OP.add), waits=[("v", e6)])
        e8 = vop(lambda e: e.tensor_scalar(uphi[:, :], ut1[:, :], CLAMP, -CLAMP,
                                           OP.min, OP.max), waits=[("v", e6)])
        e9 = vop(lambda e: e.tensor_scalar(ut2[:, :], ut2[:, :], CLAMP, -CLAMP,
                                           OP.min, OP.max), waits=[("v", e7)])
        a_sq2 = act(uphi[:, :], uphi[:, :], AF.Sin, waits=[("v", e8)])
        a_cq2 = act(uqo[:, :], ut2[:, :], AF.Sin, waits=[("v", e9)])

        v_ax = vop(lambda e, b=bAX: e.tensor_copy(ax_s[:, :], ps[b][:, 0:T]),
                   waits=[("pe", pe_ax)])
        bank_free[bAX] = [("v", v_ax)]
        we_parts = []
        for gi in range(GC):
            gs = slice(gi * 128, (gi + 1) * 128)
            g1 = vop(lambda e, b=tp_banks[gi]: e.tensor_copy(ltm[:, :],
                                                             ps[b][:, 0:8]),
                     waits=[("pe", tp_pe[gi])])
            bank_free[tp_banks[gi]] = [("v", g1)]
            g2 = vop(lambda e: e.max(srt[:, :], ltm[:, :]), waits=[("v", g1)])
            g3 = vop(lambda e: e.tensor_scalar(nm[:, :], srt[:, 0:1], -1.0, None,
                                               OP.mult), waits=[("v", g2)])
            g4 = vop(lambda e: e.tensor_scalar(msk[:, :], ltm[:, :], srt[:, 1:2],
                                               None, OP.is_ge), waits=[("v", g2)])
            g5 = act(expt[:, :], ltm[:, :], AF.Exp, bias=nm[:, 0:1],
                     waits=[("v", g3)])
            g6 = vop(lambda e: e.tensor_tensor(expt[:, :], expt[:, :], msk[:, :],
                                               op=OP.mult),
                     waits=[("a", g5), ("v", g4)])
            g7 = vop(lambda e: e.reduce_sum(den[:, :], expt[:, :], axis=AXX),
                     waits=[("v", g6)])
            g8 = vop(lambda e: e.reciprocal(wiv[:, :], den[:, :]), waits=[("v", g7)])
            g9 = vop(lambda e: e.tensor_scalar(wtm[:, :], expt[:, :], wiv[:, 0:1],
                                               2.0, OP.mult, OP.mult),
                     waits=[("v", g8)])
            bB_, wB_ = take_small()
            tpose(ps[bB_][0:8, 0:128], wtm[:, :], i128s[:, :],
                  waits=list(wB_) + [("v", g9)])
            pe_bt = S.cur("pe")
            vw = vop(lambda e, b=bB_, gs=gs: e.tensor_copy(we_s[:, gs],
                                                           ps[b][0:8, 0:128]),
                     waits=[("pe", pe_bt)])
            bank_free[bB_] = [("v", vw)]
            we_parts.append(vw)
        bRP, wRP = take_small()
        mm(ps[bRP][:, 0:T], reps[:, :], we_s[:, :], True, True,
           waits=list(wRP) + [("v", we_parts[-1])])
        pe_rep = S.cur("pe")
        v_wax = vop(lambda e, b=bRP: e.tensor_tensor(ax_s[:, :], ax_s[:, :],
                                                     ps[b][:, 0:T], op=OP.mult),
                    waits=[("pe", pe_rep), ("v", v_ax)])
        bank_free[bRP] = [("v", v_wax)]

        silu_ev = {}
        for m in range(N_OPEN):
            bank = base_banks[m]
            mm(ps[bank][:, 0:T], lb_s[:, m*128:(m+1)*128], ax_s[:, :], False, True,
               waits=[("v", v_wax)] if m == 0 else ())
            pe_g = S.cur("pe")
            ev = act(upr[:, m*T:(m+1)*T], ps[bank][:, 0:T], AF.Silu,
                     bias=bb_s[:, m:m+1], waits=[("pe", pe_g)])
            bank_free[bank] = [("a", ev)]
            silu_ev[m] = ev
        for m in range(N_OPEN, DC):
            bank = base_group(m, False)
            mm(ps[bank][:, 0:T], lb_s[:, m*128:(m+1)*128], ax_s[:, :], False, True,
               waits=[("v", v_wax)])
            pe_g = S.cur("pe")
            ev = act(upr[:, m*T:(m+1)*T], ps[bank][:, 0:T], AF.Silu,
                     bias=bb_s[:, m:m+1], waits=[("pe", pe_g)])
            bank_free[bank] = [("a", ev)]
            silu_ev[m] = ev
        a_g = S.cur("a")
        pe_base_end = S.cur("pe")
        if pad_next is not None:
            prologue_sq(pad_next, uge_hz=[("pe", pe_w2_prev)])

        vw1 = vop(lambda e: e.scalar_tensor_tensor(ut1[:, :], umag[:, :], 5.0,
                                                   ua[:, :], op0=OP.mult, op1=OP.mult),
                  waits=[("a", a_mag), ("v", e4)])
        vw2 = vop(lambda e: e.scalar_tensor_tensor(ut2[:, :], umag[:, :], 5.0,
                                                   ub[:, :], op0=OP.mult, op1=OP.mult),
                  waits=[("v", e9)])
        for j, (src_u, dst_u, crow) in enumerate(
                [(umag, uac, 16), (ut1, umr, 0), (ut2, umi, 8)]):
            for k in range(DC):
                ks = slice(k*T, (k+1)*T)
                cc = slice(crow + k, crow + k + 1)
                w = []
                if j == 0 and k == 0:
                    w = [("a", a_mag)]
                if j == 1 and k == 0:
                    w = [("v", vw2), ("pe", pe_sumsq)]
                vs = vop(lambda e, ks=ks, cc=cc, s=src_u, d=dst_u:
                         e.tensor_tensor_scan(d[:, ks], sc1[:, :], s[:, ks],
                                              carry[:, cc], OP.mult, OP.add),
                         waits=w)
                lc = slice(k*T + T - 1, k*T + T)
                vop(lambda e, lc=lc, cc=cc, d=dst_u:
                    e.tensor_copy(carry[:, cc], d[:, lc]),
                    waits=[("v", vs)])
            if j == 0:
                g_scan = S.cur("v")
        v_scan = S.cur("v")

        bS4, wS4 = take_small()
        bQ4, wQ4 = take_small()

        def ln2_stat_unit(i, u, rdy, prev, split_runs=False):
            a_s = act(ukf[:, :], u[:, :], AF.Square,
                      waits=([("pe", prev)] if prev
                             else [("pe", pe_sumsq), ("v", vB)]) + [rdy])
            runs = ([("s", range(DC)), ("q", range(DC))] if split_runs
                    else [("sq", range(DC))])
            for which, ks_ in runs:
                for k in ks_:
                    gk = i * DC + k
                    if which in ("s", "sq"):
                        w = [rdy] if k == 0 else []
                        if gk == 0:
                            w += list(wS4)
                        mm(ps[bS4][0:1, 0:T], ones[:, 0:1], u[:, k*T:(k+1)*T],
                           gk == 0, gk == 31, waits=w)
                    if which in ("q", "sq"):
                        w = [("a", a_s)] if k == 0 else []
                        if gk == 0:
                            w += list(wQ4)
                        mm(ps[bQ4][32:33, 0:T], ones[:, 0:1], ukf[:, k*T:(k+1)*T],
                           gk == 0, gk == 31, waits=w, tp=(0, 32))
            return S.cur("pe")

        pe_prev_sq = ln2_stat_unit(0, ua, ("v", vA), 0)
        pe_prev_sq = ln2_stat_unit(1, ub, ("v", vB), pe_prev_sq)

        if pad_next is not None:
            prologue_mms(pad_next)

        a_ln = act(uac[:, :], uac[:, :], AF.Ln, bias=1e-8, scale=5.0,
                   waits=[("v", g_scan)])
        a_isq = act(uac[:, :], uac[:, :], AF.Exp, scale=-0.5,
                    waits=[("a", a_ln)])
        v20 = vop(lambda e: e.tensor_tensor(umr[:, :], umr[:, :], uac[:, :],
                                            op=OP.mult),
                  waits=[("a", a_isq), ("v", v_scan)])
        v21 = vop(lambda e: e.tensor_tensor(umi[:, :], umi[:, :], uac[:, :],
                                            op=OP.mult),
                  waits=[("a", a_isq), ("v", v_scan)])

        r1 = vop(lambda e: e.tensor_tensor(ut1[:, :], umr[:, :], uqo[:, :],
                                           op=OP.mult),
                 waits=[("v", v20), ("a", a_cq2)])
        r2 = vop(lambda e: e.tensor_tensor(ut2[:, :], umi[:, :], uphi[:, :],
                                           op=OP.mult),
                 waits=[("v", v21), ("a", a_sq2), ("a", a_cq2)])
        r3 = vop(lambda e: e.tensor_tensor(uac[:, :], ut1[:, :], ut2[:, :],
                                           op=OP.add), waits=[("v", r2)])
        r4 = vop(lambda e: e.tensor_tensor(ut1[:, :], umi[:, :], uqo[:, :],
                                           op=OP.mult), waits=[("v", r3)])
        r5 = vop(lambda e: e.tensor_tensor(ut2[:, :], umr[:, :], uphi[:, :],
                                           op=OP.mult), waits=[("v", r3)])
        r6 = vop(lambda e: e.tensor_tensor(umag[:, :], ut1[:, :], ut2[:, :],
                                           op=OP.subtract),
                 waits=[("v", r5), ("v", g_scan)])
        ctx_units = [ua, ub, uac, umag]

        pe_prev_sq = ln2_stat_unit(2, uac, ("v", r3), pe_prev_sq,
                                   split_runs=True)
        pe_prev_sq = ln2_stat_unit(3, umag, ("v", r6), pe_prev_sq,
                                   split_runs=True)
        pe_stats4 = S.cur("pe")
        w1_ = vop(lambda e, b=bS4: e.tensor_scalar(stat[0:1, R_MU], ps[b][0:1, 0:T],
                                                   1.0/4096.0, None, OP.mult),
                  waits=[("pe", pe_stats4)])
        w2_ = vop(lambda e: e.tensor_tensor(stat[0:1, R_TMP], stat[0:1, R_MU],
                                            stat[0:1, R_MU], op=OP.mult),
                  waits=[("v", w1_)])
        w3_ = vop(lambda e, b=bQ4: e.scalar_tensor_tensor(stat[0:1, R_VAR],
                                                          ps[b][32:33, 0:T],
                                                          1.0/4096.0, stat[0:1, R_TMP],
                                                          op0=OP.mult, op1=OP.subtract),
                  waits=[("v", w2_)])
        bank_free[bS4] = [("v", w1_)]
        bank_free[bQ4] = [("v", w3_)]
        a2 = act(stat[0:1, R_SD], stat[0:1, R_VAR], AF.Ln, bias=1e-5,
                 waits=[("v", w3_)])
        a2b = act(stat[0:1, R_ISTD], stat[0:1, R_SD], AF.Exp, scale=-0.5,
                  waits=[("a", a2)])
        bM4, wM4 = take_small()
        mm(ps[bM4][:, 0:T], onesr[0:1, 0:128], stat[0:1, R_MU], True, True,
           waits=list(wM4) + [("v", w2_)])
        pe_bc4m = S.cur("pe")
        bI4, wI4 = take_small()
        mm(ps[bI4][:, 0:T], onesr[0:1, 0:128], stat[0:1, R_ISTD], True, True,
           waits=list(wI4) + [("a", a2b)])
        pe_bc4i = S.cur("pe")
        wc1 = vop(lambda e, b=bM4: e.tensor_copy(mu4b[:, :], ps[b][:, 0:T]),
                  waits=[("pe", pe_bc4m)])
        wc2 = vop(lambda e, b=bI4: e.tensor_copy(is4b[:, :], ps[b][:, 0:T]),
                  waits=[("pe", pe_bc4i)])
        bank_free[bM4] = [("v", wc1)]
        bank_free[bI4] = [("v", wc2)]
        split_done = {}
        wv = wg = [("v", wc2)]
        for i, u in enumerate(ctx_units):
            for k in range(DC):
                ks = slice(k*T, (k+1)*T)
                c = i * DC + k
                eng = "vector" if k < 6 else "gpsimd"
                sg = "v" if k < 6 else "g"
                if k < 6:
                    w, wv = wv, []
                else:
                    w, wg = wg, []
                c0 = S.em(eng, lambda e, u=u, ks=ks: e.tensor_tensor(
                    u[:, ks], u[:, ks], mu4b[:, :], op=OP.subtract), waits=w)
                c1x = S.em(eng, lambda e, u=u, ks=ks: e.tensor_tensor(
                    u[:, ks], u[:, ks], is4b[:, :], op=OP.mult),
                    waits=[(sg, c0)])
                cnt = act(u[:, ks], u[:, ks], AF.Identity,
                          bias=lbt_s[:, c:c+1], scale=lg_s[:, c:c+1],
                          waits=[(sg, c1x)])
                split_done[c] = ("a", cnt)
        v_ctx = S.cur("v"); g_ctx = S.cur("g"); a_ctx = S.cur("a")

        w1_dw = {i: S.cur("w1%d" % i) for i in range(5)}
        for m in range(16):
            bank, bw_ = take_big()
            for qt in range(4):
                idx = 4 * m + qt
                par = (t * 64 + idx) % 5
                for kk in range(8):
                    k = qt * 8 + kk
                    u = ctx_units[k // DC]
                    ks = slice((k % DC) * T, (k % DC + 1) * T)
                    waits = []
                    if m == 0:
                        waits.append(split_done[k])
                    if kk == 0:
                        waits.append(("w1" + str(par), w1_dw[par]))
                        if qt == 0:
                            waits += list(bw_)
                        if idx == 4:
                            waits += [("v", v_ctx), ("g", g_ctx),
                                      ("a", a_ctx)]
                    mm(ps[bank][:, 0:T], w1b[par][:, kk*128:(kk+1)*128],
                       u[:, ks], k == 0, k == 31, waits=waits)
                pe_g = S.cur("pe")
                last_use[("w1", par)] = ("pe", pe_g)
                nidx = idx + 5
                npar = (t * 64 + nidx) % 5
                if nidx < 64:
                    nm_, nq = nidx // 4, nidx % 4
                    w1_dw[npar] = stream_dma(
                        "w1", npar, w1b[npar][:, :],
                        d_w1[nm_, :, nq*1024:(nq+1)*1024])
                elif t + 1 < NT and nidx < 69:
                    nq = nidx - 64
                    w1_dw[npar] = stream_dma(
                        "w1", npar, w1b[npar][:, :],
                        d_w1[nq // 4, :, (nq % 4)*1024:(nq % 4 + 1)*1024])
            ev = act(uge[:, m*T:(m+1)*T], ps[bank][:, 0:T],
                     AF.Gelu, bias=b1_s[:, m:m+1], waits=[("pe", S.cur("pe"))])
            bank_free[bank] = [("a", ev)]
        a_gelu = S.cur("a")

        if pad_next is not None:
            pro_next = prologue_b2(pad_next,
                                   row_extra=[("pe", pe_bc4i), ("a", a2b)])
            xn_next = emit_xn(pro_next, xn_extra=xn_hazard[(t + 1) % 2])
            phi_next = emit_phi(pro_next["pe_sumsq"],
                                waits0=[("g", pro_next["g_iota"]), ("v", r5)],
                                kf_extra=[("pe", pe_prev_sq)])
        else:
            pro_next = None
            xn_next = None
            phi_next = None

        w2_dw = {i: S.cur("w2%d" % i) for i in range(3)}
        for m in range(8):
            bank, bw_ = take_big()
            for q in range(4):
                idx = 4 * m + q
                par = (t * 32 + idx) % 3
                for kk in range(4):
                    k = q * 4 + kk
                    waits = []
                    if kk == 0:
                        waits.append(("w2" + str(par), w2_dw[par]))
                        if q == 0:
                            waits += list(bw_)
                        if idx == 0:
                            waits.append(("a", a_gelu))
                    mm(ps[bank][:, 0:T], w2b[par][:, kk*128:(kk+1)*128],
                       uge[:, k*T:(k+1)*T], k == 0, k == 15, waits=waits)
                pe_g = S.cur("pe")
                last_use[("w2", par)] = ("pe", pe_g)
                nidx = idx + 3
                npp = (t * 32 + nidx) % 3
                if nidx < 32:
                    nm_, nq = nidx // 4, nidx % 4
                    w2_dw[npp] = stream_dma(
                        "w2", npp, w2b[npp][:, :],
                        d_w2[nm_, :, nq*512:(nq+1)*512])
                elif t + 1 < NT and nidx < 35:
                    nq = nidx - 32
                    w2_dw[npp] = stream_dma(
                        "w2", npp, w2b[npp][:, :],
                        d_w2[0, :, nq*512:(nq+1)*512])
            mk = slice(m*T, (m+1)*T)
            ev = vop(lambda e, b=bank, mk=mk, m=m, u=uxn_t:
                     e.scalar_tensor_tensor(uqo[:, mk], ps[b][:, 0:T],
                                            b2_s[:, m:m+1], u[:, mk],
                                            op0=OP.add, op1=OP.add),
                     waits=[("pe", S.cur("pe"))])
            bank_free[bank] = [("v", ev)]
        vs_v = S.cur("v")
        pe_w2_prev = S.cur("pe")
        for k in range(DC):
            ks = slice(k*T, (k+1)*T)
            eng = "vector" if k < 6 else "gpsimd"
            w = ([("a", a_g), ("v", vs_v)] if k in (0, 6) else [])
            S.em(eng, lambda e, ks=ks: e.tensor_tensor(
                ut1[:, ks], uqo[:, ks], upr[:, ks], op=OP.mult), waits=w)
        f1_v = S.cur("v"); f1_g = S.cur("g")
        for k in range(DC):
            ks = slice(k*T, (k+1)*T)
            eng = "vector" if k < 6 else "gpsimd"
            w = ([("v", f1_v)] if k == 0 else
                 ([("g", f1_g)] if k == 6 else []))
            S.em(eng, lambda e, ks=ks, uxp=uxp: e.tensor_tensor(
                uxp[:, ks], uxp[:, ks], ut1[:, ks], op=OP.add), waits=w)
        f2_v = S.cur("v"); f2_g = S.cur("g")

        osrc = bass.AP(uxp, 0, [[U, 128], [T, 8], [1, T]])
        odst = bass.AP(d_out, t0, [[2048, 128], [128 * 2048, 8], [1, T]])
        gdma(odst, osrc, waits=[("v", f2_v), ("g", f2_g)], sem="o" + str(t % 2))
        xn_hazard[t % 2] = [("v", vs_v), ("pe", pe_base_end)]
        S.barrier(engines=("vector", "scalar", "gpsimd"))
        pro = pro_next
        xn_c = xn_next
        phi_c = phi_next

    S.em("gpsimd", lambda e: e.engine_nop(),
         waits=[("o0", 16 * (NT // 2)), ("o1", 16 * (NT // 2))])

    with nc.Block() as block:
        def mk(name):
            def run(eng):
                for (e, waits, fn, sem, amt) in S.items:
                    if e != name:
                        continue
                    for sname, val in waits:
                        eng.wait_ge(sems[sname], val)
                    ins = fn(eng)
                    ins.then_inc(sems[sem], amt)
            return run
        block.sync(mk("sync"))
        block.vector(mk("vector"))
        block.scalar(mk("scalar"))
        block.tensor(mk("tensor"))
        block.gpsimd(mk("gpsimd"))

    es.close()
    return nc


def _stage_lhsT(w, K, M):
    return np.ascontiguousarray(
        w.reshape(M // 128, 128, K // 128, 128).transpose(0, 3, 2, 1))


def kernel(L0_attn_x, L0_attn_random_phases, L0_attn_W_mag, L0_attn_b_mag,
           L0_attn_W_q, L0_attn_b_q, L0_attn_ln_g, L0_attn_ln_b,
           L0_attn_W1, L0_attn_b1, L0_attn_W2, L0_attn_b2,
           L0_ffn_base_w, L0_ffn_base_b, L0_ffn_lora_A, L0_ffn_lora_B,
           L0_ffn_router_w, L0_ffn_router_b):
    f = np.asarray
    x = f(L0_attn_x, dtype=np.float32)

    rt = f(L0_ffn_router_w, dtype=np.float32).T
    rws = np.zeros((128, 64), dtype=np.float32)
    for k in range(8):
        rws[:, k*8:(k+1)*8] = rt[k*128:(k+1)*128, :]

    common = {
        "wmag": _stage_lhsT(f(L0_attn_W_mag, dtype=np.float32), 1024, 1024)
                .reshape(8, 128, 1024),
        "wqm": _stage_lhsT(f(L0_attn_W_q, dtype=np.float32), 1024, 1024)
               .reshape(8, 128, 1024),
        "bwm": _stage_lhsT(f(L0_ffn_base_w, dtype=np.float32), 1024, 1024)
               .reshape(8, 128, 1024),
        "w1m": _stage_lhsT(f(L0_attn_W1, dtype=np.float32), 4096, 2048)
               .reshape(16, 128, 4096),
        "w2m": _stage_lhsT(f(L0_attn_W2, dtype=np.float32), 2048, 1024)
               .reshape(8, 128, 2048),
        "lam": np.ascontiguousarray(
            f(L0_ffn_lora_A, dtype=np.float32).transpose(1, 0, 2).reshape(1024, 128)
            .reshape(8, 128, 128).transpose(1, 0, 2).reshape(128, 1024)),
        "lbm": np.ascontiguousarray(
            f(L0_ffn_lora_B, dtype=np.float32).reshape(128, 1024)),
        "rwm": rws,
        "bmag": np.ascontiguousarray(f(L0_attn_b_mag, np.float32).reshape(8, 128).T),
        "bq": np.ascontiguousarray(f(L0_attn_b_q, np.float32).reshape(8, 128).T),
        "b1": np.ascontiguousarray(f(L0_attn_b1, np.float32).reshape(16, 128).T),
        "b2": np.ascontiguousarray(f(L0_attn_b2, np.float32).reshape(8, 128).T),
        "bb": np.ascontiguousarray(f(L0_ffn_base_b, np.float32).reshape(8, 128).T),
        "lg": np.ascontiguousarray(f(L0_attn_ln_g, np.float32).reshape(32, 128).T),
        "lbt": np.ascontiguousarray(f(L0_attn_ln_b, np.float32).reshape(32, 128).T),
        "rb": np.ascontiguousarray(f(L0_ffn_router_b, np.float32).reshape(8, 1)),
        "ph": np.ascontiguousarray(
            f(L0_attn_random_phases, np.float32).reshape(8, 128).T),
        "i128": np.eye(128, dtype=np.float32),
        "i8": np.eye(8, dtype=np.float32),
        "rep": np.ascontiguousarray(
            (np.arange(128)[None, :] // 16 == np.arange(8)[:, None])
            .astype(np.float32)),
    }

    in_maps = []
    for c in range(8):
        m = dict(common)
        m["x"] = np.ascontiguousarray(x[c].T)
        in_maps.append(m)

    nc = build_program()
    res = run_bass_kernel_spmd(nc, in_maps, core_ids=list(range(8)))
    out = np.stack([np.ascontiguousarray(r["o"].T) for r in res.results], axis=0)
    return out.astype(np.float32)
